# version 27
# baseline (speedup 1.0000x reference)
import math
import numpy as np
import concourse.bass as bass
import concourse.mybir as mybir
from concourse.bass_utils import run_bass_kernel_spmd
from concourse.aot_env import allow_ici_all_to_all
from contextlib import ExitStack

F32 = mybir.dt.float32
BF16 = mybir.dt.bfloat16
F16 = mybir.dt.float16
I32 = mybir.dt.int32
AF = mybir.ActivationFunctionType
ALU = mybir.AluOpType

NC = 8
D = 2048
NH = 16
DH = 128
HPC = NH // NC
II = 5632
GU_W = 32
DN_W = 88

QUEUES = ("pe", "act", "dve", "pool", "sp")


def _th(x):
    t = getattr(x, "tensor", None)
    if t is None:
        t = x
    return t.name


class Op:
    __slots__ = ("q", "fn", "waits", "token", "inc")

    def __init__(self, q, fn, waits, token, inc):
        self.q, self.fn, self.waits, self.token, self.inc = q, fn, waits, token, inc


class Fw:
    N_DMA_LANES = 8

    def __init__(self):
        self.ops = {q: [] for q in QUEUES}
        self.counts = {}
        self.tstate = {}
        self.satisfied = {q: {} for q in QUEUES}
        self.sem_names = set()
        self.dma_rr = {q: 0 for q in QUEUES}
        self.cc_idx = 0

    def _sem_for(self, q, kind):
        if kind == "cc":
            name = f"cc{self.cc_idx}"
            self.cc_idx += 1
        elif kind == "dma":
            lane = self.dma_rr[q] % self.N_DMA_LANES
            self.dma_rr[q] += 1
            name = f"dma_{q}_{lane}"
        else:
            name = q
        self.sem_names.add(name)
        return name

    def _dep_tokens(self, reads, writes):
        deps = []
        for ap_ in reads:
            st = self.tstate.get(_th(ap_))
            if st and st[0]:
                deps.append(st[0])
        for ap_ in writes:
            st = self.tstate.get(_th(ap_))
            if st:
                if st[0]:
                    deps.append(st[0])
                deps.extend(st[1])
        return deps

    def _record(self, q, fn, reads, writes, kind, inc):
        deps = self._dep_tokens(reads, writes)
        sname = self._sem_for(q, kind)
        if kind != "compute":
            prev = self.counts.get(sname, 0)
            if prev:
                deps.append((sname, prev))
        sat = self.satisfied[q]
        waits = []
        best = {}
        for (s, v) in deps:
            if v > best.get(s, 0):
                best[s] = v
        for s, v in sorted(best.items()):
            if sat.get(s, 0) < v:
                waits.append((s, v))
                sat[s] = v
        self.counts[sname] = self.counts.get(sname, 0) + inc
        token = (sname, self.counts[sname])
        self.ops[q].append(Op(q, fn, waits, token, inc))
        for ap_ in reads:
            st = self.tstate.setdefault(_th(ap_), [None, []])
            st[1].append(token)
        for ap_ in writes:
            self.tstate[_th(ap_)] = [token, []]
        return token

    def op(self, q, fn, reads=(), writes=()):
        return self._record(q, fn, reads, writes, "compute", 1)

    def dma(self, q, out, in_, **kw):
        return self._record(q, lambda e: e.dma_start(out=out, in_=in_, **kw),
                            [in_], [out], "dma", 16)

    def dma_t(self, q, out, in_):
        return self._record(q, lambda e: e.dma_start(out=out, in_=in_,
                                                     transpose=True),
                            [in_], [out], "dma", 16)

    def cc(self, q, fn, ins, outs):
        return self._record(q, fn, list(ins), list(outs), "cc", 1)

    def wait_all(self, q):
        sat = self.satisfied[q]
        waits = []
        for s, v in sorted(self.counts.items()):
            if sat.get(s, 0) < v:
                waits.append((s, v))
                sat[s] = v
        if waits:
            self.ops[q].append(Op(q, None, waits, None, 0))

    def phase_barrier(self):
        for q in QUEUES:
            self.wait_all(q)

    def prune_incs(self):
        awaited = {}
        for q in QUEUES:
            for op_ in self.ops[q]:
                for (s, v) in op_.waits:
                    awaited.setdefault(s, set()).add(v)
        remap = {}
        keep = {}
        for s in list(self.counts):
            if s.startswith("dma_") or s.startswith("cc"):
                continue
            want = awaited.get(s, set())
            m = {}
            newc = 0
            kept = set()
            for old in range(1, self.counts[s] + 1):
                if old in want:
                    newc += 1
                    kept.add(old)
                m[old] = newc
            remap[s] = m
            keep[s] = kept
        for q in QUEUES:
            for op_ in self.ops[q]:
                op_.waits = [(s, remap[s][v] if s in remap else v)
                             for (s, v) in op_.waits]
                if op_.token is not None and op_.token[0] in remap:
                    s, v = op_.token
                    op_.inc = 1 if v in keep[s] else 0

    def emit_queue(self, q, eng, sems):
        for op_ in self.ops[q]:
            for (s, v) in op_.waits:
                eng.wait_ge(sems[s], v)
            if op_.fn is not None:
                ins_ = op_.fn(eng)
                if op_.token is None:
                    pass
                elif op_.token[0].startswith("cc"):
                    ins_.then_inc(sems[op_.token[0]])
                elif op_.inc:
                    ins_.then_inc(sems[op_.token[0]], op_.inc)


def build_program(S, dbg=False):
    TLOC = S // NC
    NTT = TLOC // 128
    NQB = S // 256
    NTB = S // 512
    TW = min(512, TLOC)
    NT5 = TLOC // TW
    ISQ = 1.0 / math.sqrt(DH)
    RG = [list(range(NC))]

    nc = bass.Bass("TRN2", target_bir_lowering=False, debug=False)

    ap = lambda t: t.ap()
    x_in = ap(nc.dram_tensor("x_in", [TLOC, D], F32, kind="ExternalInput"))
    wq_in = ap(nc.dram_tensor("wq_in", [HPC, D, DH], F32, kind="ExternalInput"))
    wk_in = ap(nc.dram_tensor("wk_in", [HPC, D, DH], F32, kind="ExternalInput"))
    wv_in = ap(nc.dram_tensor("wv_in", [HPC, D, DH], F32, kind="ExternalInput"))
    wo_in = ap(nc.dram_tensor("wo_in", [NH, DH, D], F32, kind="ExternalInput"))
    bq_in = ap(nc.dram_tensor("bq_in", [HPC, DH], F32, kind="ExternalInput"))
    bk_in = ap(nc.dram_tensor("bk_in", [HPC, DH], F32, kind="ExternalInput"))
    bv_in = ap(nc.dram_tensor("bv_in", [HPC, DH], F32, kind="ExternalInput"))
    bo_in = ap(nc.dram_tensor("bo_in", [1, D], F32, kind="ExternalInput"))
    gup_in = ap(nc.dram_tensor("gup_in", [128, 88, GU_W], I32, kind="ExternalInput"))
    gus_in = ap(nc.dram_tensor("gus_in", [128, 88, 256], F16, kind="ExternalInput"))
    dnp_in = ap(nc.dram_tensor("dnp_in", [128, 16, DN_W], I32, kind="ExternalInput"))
    dns_in = ap(nc.dram_tensor("dns_in", [128, 16, 768], F16, kind="ExternalInput"))
    out_t = ap(nc.dram_tensor("out", [TLOC, D], F32, kind="ExternalOutput"))

    ag1_in = ap(nc.dram_tensor("ag1_in", [D, TLOC], BF16))
    ag1_out = ap(nc.dram_tensor("ag1_out", [NC * D, TLOC], BF16))
    a2a_in = ap(nc.dram_tensor("a2a_in", [NC * HPC * DH, TLOC], BF16))
    a2a_out = ap(nc.dram_tensor("a2a_out", [NC * HPC * DH, TLOC], BF16))
    ag3g_in = ap(nc.dram_tensor("ag3g_in", [256, 2 * II], BF16))
    ag3g_out = ap(nc.dram_tensor("ag3g_out", [NC * 256, 2 * II], BF16))
    ag3d_in = ap(nc.dram_tensor("ag3d_in", [768, D], BF16))
    ag3d_out = ap(nc.dram_tensor("ag3d_out", [NC * 768, D], BF16))
    x1_park = ap(nc.dram_tensor("x1_park", [TLOC, D], F32))
    dbg_qkv = ap(nc.dram_tensor("dbg_qkv", [3 * 128, S], BF16))
    dbg_sp = ap(nc.dram_tensor("dbg_sp", [3 * 128, 512], F32))
    dbg_z0 = ap(nc.dram_tensor("dbg_z0", [128, 256], BF16))
    dbg_pq = ap(nc.dram_tensor("dbg_pq", [3 * 128, 512], BF16))
    ln2_park = ap(nc.dram_tensor("ln2_park", [D, TLOC], BF16))
    y_park = ap(nc.dram_tensor("y_park", [TLOC, D], F32))

    fw = Fw()
    es = ExitStack()

    def sbuf(scope, name, shape, dtype):
        return scope.enter_context(nc.sbuf_tensor(name, shape, dtype))[:]

    def psum(scope, name, shape, dtype=F32):
        return scope.enter_context(nc.psum_tensor(name, shape, dtype))[:]

    c_eps5 = sbuf(es, "c_eps5", [128, 1], F32)
    c_eps6 = sbuf(es, "c_eps6", [128, 1], F32)
    c_nh = sbuf(es, "c_nh", [128, 1], F32)
    ones_f = sbuf(es, "ones_f", [1, 128], F32)
    mask_t = sbuf(es, "mask_t", [128, 128], F32)
    maskT_t = sbuf(es, "maskT_t", [128, 128], F32)
    iota_r = sbuf(es, "iota_r", [128, 128], F32)
    rowi = sbuf(es, "rowi", [128, 1], F32)
    bo_bc = sbuf(es, "bo_bc", [128, D], F32)
    bqs = sbuf(es, "bqs", [128, HPC], F32)
    bks = sbuf(es, "bks", [128, HPC], F32)
    bqr = sbuf(es, "bqr", [128, HPC], F32)
    bkr = sbuf(es, "bkr", [128, HPC], F32)
    bvs = sbuf(es, "bvs", [128, HPC], F32)
    c_isq = sbuf(es, "c_isq", [128, 1], F32)
    ident = sbuf(es, "ident", [128, 128], BF16)

    fw.op("dve", lambda e: e.memset(c_eps5, 1e-5), writes=[c_eps5])
    fw.op("dve", lambda e: e.memset(c_eps6, 1e-6), writes=[c_eps6])
    fw.op("dve", lambda e: e.memset(c_nh, -0.5), writes=[c_nh])
    fw.op("dve", lambda e: e.memset(c_isq, ISQ), writes=[c_isq])
    fw.op("dve", lambda e: e.memset(ones_f, 1.0), writes=[ones_f])
    fw.op("pool", lambda e: e.iota(iota_r, pattern=[[1, 128]], base=0,
          channel_multiplier=0, allow_small_or_imprecise_dtypes=True),
          writes=[iota_r])
    fw.op("pool", lambda e: e.iota(rowi, pattern=[[0, 1]], base=0,
          channel_multiplier=1, allow_small_or_imprecise_dtypes=True),
          writes=[rowi])
    fw.op("dve", lambda e: e.tensor_scalar(out=mask_t, in0=iota_r,
          scalar1=rowi, scalar2=-100000.0, op0=ALU.is_gt, op1=ALU.mult),
          reads=[iota_r, rowi], writes=[mask_t])
    fw.op("dve", lambda e: e.tensor_scalar(out=maskT_t, in0=iota_r,
          scalar1=rowi, scalar2=-100000.0, op0=ALU.is_lt, op1=ALU.mult),
          reads=[iota_r, rowi], writes=[maskT_t])
    fw.op("dve", lambda e: e.tensor_scalar(out=ident, in0=iota_r,
          scalar1=rowi, scalar2=None, op0=ALU.is_equal),
          reads=[iota_r, rowi], writes=[ident])

    fw.dma("sp", bqr, bq_in.rearrange("h e -> e h"),
           allow_slow_non_contiguous=True)
    fw.dma("sp", bkr, bk_in.rearrange("h e -> e h"),
           allow_slow_non_contiguous=True)
    fw.dma("sp", bvs, bv_in.rearrange("h e -> e h"),
           allow_slow_non_contiguous=True)
    fw.op("dve", lambda e: e.tensor_scalar(out=bqs, in0=bqr, scalar1=ISQ,
          scalar2=None, op0=ALU.mult), reads=[bqr], writes=[bqs])
    fw.op("dve", lambda e: e.tensor_copy(out=bks, in_=bkr),
          reads=[bkr], writes=[bks])

    with es:
      ph0 = ExitStack()
      with ph0:
        ps_a = psum(ph0, "ps_a", [128, 512], F32)
        ps_b = psum(ph0, "ps_b", [128, 512], F32)
        bo_row = sbuf(ph0, "bo_row", [1, D], F32)
        fw.dma("sp", bo_row, bo_in)
        for c4 in range(4):
            fw.op("pe", lambda e, c4=c4: e.matmul(
                ps_a[:, 0:512], ones_f, bo_row[:, 512 * c4:512 * (c4 + 1)],
                start=True, stop=True), reads=[ones_f, bo_row], writes=[ps_a])
            fw.op("dve", lambda e, c4=c4: e.tensor_copy(
                out=bo_bc[:, 512 * c4:512 * (c4 + 1)], in_=ps_a[:, 0:512]),
                reads=[ps_a], writes=[bo_bc])


        xa = [sbuf(ph0, f"xa{i}", [128, D], F32) for i in range(2)]
        lnt = [sbuf(ph0, f"lnt{i}", [128, D], BF16) for i in range(2)]
        lnF = [sbuf(ph0, f"lnF{i}", [128, D], BF16) for i in range(2)]
        ps_T = [psum(ph0, f"ps_T{i}", [128, 128], BF16) for i in range(2)]
        bn6 = sbuf(ph0, "bn6", [128, 24], F32)
        bn2 = sbuf(ph0, "bn2", [128, 2], F32)
        st_ve = sbuf(ph0, "st_ve", [128, 1], F32)
        st_rs = sbuf(ph0, "st_rs", [128, 1], F32)
        st_nm = sbuf(ph0, "st_nm", [128, 1], F32)

        def ln_stats(src, eps_tile, b6, b2, ve, rs_, nm):
            for c in range(4):
                fw.op("dve", lambda e, c=c: e.bn_stats(
                    out=b6[:, 6 * c:6 * c + 6],
                    in_=src[:, 512 * c:512 * (c + 1)]), reads=[src], writes=[b6])
            fw.op("dve", lambda e: e.bn_aggr(out=b2, in_=b6),
                  reads=[b6], writes=[b2])
            fw.op("act", lambda e: e.activation(out=ve, in_=b2[:, 1:2],
                  func=AF.Ln, bias=eps_tile, scale=1.0),
                  reads=[b2, eps_tile], writes=[ve])
            fw.op("act", lambda e: e.activation(out=rs_, in_=ve, func=AF.Exp,
                  bias=0.0, scale=c_nh), reads=[ve, c_nh], writes=[rs_])
            fw.op("dve", lambda e: e.scalar_tensor_tensor(out=nm,
                  in0=b2[:, 0:1], scalar=-1.0, in1=rs_,
                  op0=ALU.mult, op1=ALU.mult), reads=[b2, rs_], writes=[nm])

        for tt in range(NTT):
            b = tt % 2
            fw.dma("sp", xa[b], x_in[128 * tt:128 * (tt + 1), :])
            ln_stats(xa[b], c_eps5, bn6, bn2, st_ve, st_rs, st_nm)
            fw.op("dve", lambda e, b=b: e.tensor_scalar(out=lnt[b], in0=xa[b],
                  scalar1=st_rs, scalar2=st_nm, op0=ALU.mult, op1=ALU.add),
                  reads=[xa[b], st_rs, st_nm], writes=[lnt[b]])
            for k in range(16):
                fw.op("pe", lambda e, b=b, k=k: e.transpose(
                    ps_T[k % 2], lnt[b][:, 128 * k:128 * (k + 1)], ident),
                    reads=[lnt[b], ident], writes=[ps_T[k % 2]])
                eng = ("dve", "act")[k % 2]
                if eng == "dve":
                    fw.op("dve", lambda e, b=b, k=k: e.tensor_copy(
                        out=lnF[b][:, 128 * k:128 * (k + 1)], in_=ps_T[k % 2]),
                        reads=[ps_T[k % 2]], writes=[lnF[b]])
                else:
                    fw.op("act", lambda e, b=b, k=k: e.copy(
                        out=lnF[b][:, 128 * k:128 * (k + 1)], in_=ps_T[k % 2]),
                        reads=[ps_T[k % 2]], writes=[lnF[b]])
            fw.dma("act", ag1_in.rearrange("(k p) t -> p k t", p=128)[
                :, :, 128 * tt:128 * (tt + 1)],
                   lnF[b].rearrange("p (k t) -> p k t", k=16))

        fw.cc("pool", lambda e: e.collective_compute(
            "AllGather", ALU.bypass, replica_groups=RG,
            ins=[ag1_in.opt()], outs=[ag1_out.opt()]), [ag1_in], [ag1_out])
      fw.phase_barrier()

      phB = ExitStack()
      with phB:
        scr = sbuf(phB, "scr", [128, S], BF16)
        Qfm = sbuf(phB, "Qfm", [128, S], BF16)
        Kfm = sbuf(phB, "Kfm", [128, S], BF16)
        onescol = sbuf(phB, "onescol", [128, 1], BF16)
        vfm_sb = sbuf(phB, "vfm_sb", [128, 512], BF16)
        Vtm = sbuf(phB, "Vtm", [128, S], BF16)
        PT_sb = [sbuf(phB, f"PT{i}", [128, 512], BF16) for i in range(3)]
        z_sb = [sbuf(phB, f"zsb{i}", [128, 512], BF16) for i in range(2)]
        mkc = sbuf(phB, "mkc", [1, 16], F32)
        mqc = sbuf(phB, "mqc", [1, 16], F32)
        mk1 = sbuf(phB, "mk1", [1, 1], F32)
        mq1 = sbuf(phB, "mq1", [1, 1], F32)
        lnmk = sbuf(phB, "lnmk", [1, 1], F32)
        nc0 = sbuf(phB, "nc0", [1, 1], F32)
        c0col = sbuf(phB, "c0col", [128, 1], F32)
        drow = sbuf(phB, "drow", [1, 512], F32)
        drec = sbuf(phB, "drec", [1, 512], F32)
        dinvb = sbuf(phB, "dinvb", [128, 512], F32)
        zsum_sb = sbuf(phB, "zsum_sb", [128, 512], F32)
        c_half = sbuf(phB, "c_half", [128, 1], F32)
        wqkv = [sbuf(phB, f"wqkv{i}", [128, 3 * 16 * 128], BF16) for i in range(2)]
        ln1t = [[sbuf(phB, f"ln1t{bb}_{k}", [128, 512], BF16) for k in range(16)]
                for bb in range(2)]
        ps_S = [psum(phB, f"ps_S{i}", [128, 512], F32) for i in range(2)]
        ps_ST = ps_S
        ps_Z = [psum(phB, f"ps_Z{i}", [128, 512], F32) for i in range(2)]
        ps_D = psum(phB, "ps_D", [1, 512], F32)
        ps_BC = psum(phB, "ps_BC", [128, 512], F32)
        ps_DQ = [psum(phB, f"ps_DQ{i}", [128, 128], BF16) for i in range(2)]

        dbg_p_sb = sbuf(phB, "dbg_p_sb", [128, 512], BF16)
        GNT = 8
        DNT = 2
        dq_pk = sbuf(phB, "dq_pk", [128, GNT * GU_W], I32)
        dq_sc = sbuf(phB, "dq_sc", [128, GNT * 256], F16)
        dq_st = sbuf(phB, "dq_st", [128, GNT * 256], BF16)
        dq_t0 = sbuf(phB, "dq_t0", [128, GNT * GU_W], I32)
        dq_t1 = sbuf(phB, "dq_t1", [128, GNT * GU_W], I32)
        dq_t2 = sbuf(phB, "dq_t2", [128, GNT * GU_W], I32)
        dq_t3 = sbuf(phB, "dq_t3", [128, GNT * GU_W], I32)
        dqT_st = sbuf(phB, "dqT_st", [128, 2 * GNT * 128], BF16)
        dnT_st = sbuf(phB, "dnT_st", [128, 6 * DNT * 128], BF16)
        dn_pk = sbuf(phB, "dn_pk", [128, DNT * DN_W], I32)
        dn_sc = sbuf(phB, "dn_sc", [128, DNT * 768], F16)
        dn_st = sbuf(phB, "dn_st", [128, DNT * 768], BF16)

        def bitchain(src_, n_el, j):
            fw.op("dve", lambda e: e.tensor_scalar(out=dq_t0[:, 0:n_el],
                  in0=src_, scalar1=4 * j, scalar2=15,
                  op0=ALU.logical_shift_right, op1=ALU.bitwise_and),
                  reads=[src_], writes=[dq_t0])
            fw.op("dve", lambda e: e.tensor_scalar(out=dq_t3[:, 0:n_el],
                  in0=dq_t0[:, 0:n_el], scalar1=7, scalar2=None,
                  op0=ALU.bitwise_and),
                  reads=[dq_t0], writes=[dq_t3])
            fw.op("dve", lambda e: e.tensor_scalar(out=dq_t1[:, 0:n_el],
                  in0=dq_t3[:, 0:n_el], scalar1=2, scalar2=None,
                  op0=ALU.is_ge), reads=[dq_t3], writes=[dq_t1])
            fw.op("dve", lambda e: e.tensor_tensor(out=dq_t1[:, 0:n_el],
                  in0=dq_t3[:, 0:n_el], in1=dq_t1[:, 0:n_el], op=ALU.add),
                  reads=[dq_t3, dq_t1], writes=[dq_t1])
            fw.op("dve", lambda e: e.tensor_scalar(out=dq_t2[:, 0:n_el],
                  in0=dq_t3[:, 0:n_el], scalar1=1, scalar2=None,
                  op0=ALU.is_ge), reads=[dq_t3], writes=[dq_t2])
            fw.op("dve", lambda e: e.scalar_tensor_tensor(out=dq_t1[:, 0:n_el],
                  in0=dq_t1[:, 0:n_el], scalar=251, in1=dq_t2[:, 0:n_el],
                  op0=ALU.add, op1=ALU.mult),
                  reads=[dq_t1, dq_t2], writes=[dq_t1])
            fw.op("dve", lambda e: e.tensor_scalar(out=dq_t1[:, 0:n_el],
                  in0=dq_t1[:, 0:n_el], scalar1=22, scalar2=None,
                  op0=ALU.logical_shift_left), reads=[dq_t1], writes=[dq_t1])
            fw.op("dve", lambda e: e.tensor_scalar(out=dq_t2[:, 0:n_el],
                  in0=dq_t0[:, 0:n_el], scalar1=3, scalar2=31,
                  op0=ALU.logical_shift_right, op1=ALU.logical_shift_left),
                  reads=[dq_t0], writes=[dq_t2])
            fw.op("dve", lambda e: e.tensor_tensor(out=dq_t1[:, 0:n_el],
                  in0=dq_t1[:, 0:n_el], in1=dq_t2[:, 0:n_el],
                  op=ALU.bitwise_or), reads=[dq_t1, dq_t2], writes=[dq_t1])

        pending_T = []

        def gu_chunk(ci):
            nt = GNT
            fw.dma("pool", dq_pk, gup_in[:, nt * ci:nt * (ci + 1), :]
                   .rearrange("p t w -> p (t w)"))
            fw.dma("pool", dq_sc, gus_in[:, nt * ci:nt * (ci + 1), :]
                   .rearrange("p t d -> p (t d)"))
            yield
            for j in range(8):
                bitchain(dq_pk, nt * GU_W, j)
                st3 = dq_st.rearrange("p (t d) -> p t d", t=nt)[:, :, j::8]
                sc3 = dq_sc.rearrange("p (t d) -> p t d", t=nt)[:, :, j::8]
                b3 = dq_t1[:, 0:nt * GU_W].rearrange("p (t w) -> p t w", t=nt)
                fw.op("pool", lambda e, st3=st3, sc3=sc3, b3=b3: e.tensor_tensor(
                      out=st3, in0=b3.bitcast(F32), in1=sc3, op=ALU.mult),
                      reads=[dq_t1, dq_sc], writes=[dq_st])
                yield

            def emit_T(ci=ci, nt=nt):
                for t_ in range(nt):
                    for dc in range(2):
                        fw.op("pe", lambda e, t_=t_, dc=dc: e.transpose(
                            ps_DQ[(t_ * 2 + dc) % 2],
                            dq_st.rearrange("p (t d) -> p t d", t=nt)[
                                :, t_, 128 * dc:128 * (dc + 1)], ident),
                            reads=[dq_st, ident],
                            writes=[ps_DQ[(t_ * 2 + dc) % 2]])
                        fw.op("act", lambda e, t_=t_, dc=dc: e.copy(
                            out=dqT_st.rearrange("p (dc t o) -> p dc t o",
                                                 dc=2, t=nt)[:, dc, t_, :],
                            in_=ps_DQ[(t_ * 2 + dc) % 2]),
                            reads=[ps_DQ[(t_ * 2 + dc) % 2]], writes=[dqT_st])
                fw.dma("pool", ag3g_in.rearrange("(dc p) o -> p dc o", p=128)[
                    :, :, nt * 128 * ci:nt * 128 * (ci + 1)]
                    .rearrange("p dc (t o) -> p dc t o", t=nt),
                       dqT_st.rearrange("p (dc t o) -> p dc t o", dc=2, t=nt))
            pending_T.append(emit_T)

        def dn_chunk(ci):
            nt = DNT
            fw.dma("pool", dn_pk, dnp_in[:, nt * ci:nt * (ci + 1), :]
                   .rearrange("p t w -> p (t w)"))
            fw.dma("pool", dn_sc, dns_in[:, nt * ci:nt * (ci + 1), :]
                   .rearrange("p t d -> p (t d)"))
            fw.op("pool", lambda e: e.memset(dn_st, 0.0), writes=[dn_st])
            yield
            for j in range(8):
                bitchain(dn_pk, nt * DN_W, j)
                st3 = dn_st.rearrange("p (t d) -> p t d", t=nt)[:, :, 0:704][:, :, j::8]
                sc3 = dn_sc.rearrange("p (t d) -> p t d", t=nt)[:, :, 0:704][:, :, j::8]
                b3 = dq_t1[:, 0:nt * DN_W].rearrange("p (t w) -> p t w", t=nt)
                fw.op("pool", lambda e, st3=st3, sc3=sc3, b3=b3: e.tensor_tensor(
                      out=st3, in0=b3.bitcast(F32), in1=sc3, op=ALU.mult),
                      reads=[dq_t1, dn_sc], writes=[dn_st])
                yield

            def emit_T(ci=ci, nt=nt):
                for t_ in range(nt):
                    for ib in range(6):
                        fw.op("pe", lambda e, t_=t_, ib=ib: e.transpose(
                            ps_DQ[(t_ * 6 + ib) % 2],
                            dn_st.rearrange("p (t d) -> p t d", t=nt)[
                                :, t_, 128 * ib:128 * (ib + 1)], ident),
                            reads=[dn_st, ident],
                            writes=[ps_DQ[(t_ * 6 + ib) % 2]])
                        fw.op("act", lambda e, t_=t_, ib=ib: e.copy(
                            out=dnT_st.rearrange("p (ib t o) -> p ib t o",
                                                 ib=6, t=nt)[:, ib, t_, :],
                            in_=ps_DQ[(t_ * 6 + ib) % 2]),
                            reads=[ps_DQ[(t_ * 6 + ib) % 2]], writes=[dnT_st])
                fw.dma("pool", ag3d_in.rearrange("(ib p) o -> p ib o", p=128)[
                    :, :, nt * 128 * ci:nt * 128 * (ci + 1)]
                    .rearrange("p ib (t o) -> p ib t o", t=nt),
                       dnT_st.rearrange("p (ib t o) -> p ib t o", ib=6, t=nt))
            pending_T.append(emit_T)

        fw.op("dve", lambda e: e.memset(onescol, 1.0), writes=[onescol])
        fw.op("dve", lambda e: e.memset(c_half, 0.5), writes=[c_half])
        deq_jobs = [gu_chunk(ci) for ci in range(88 // GNT)]
        deq_jobs += [dn_chunk(ci) for ci in range(16 // DNT)]
        deq_done = [False]

        def flush_T():
            while pending_T:
                pending_T.pop(0)()

        def deq_step(n=1):
            for _ in range(n):
                while deq_jobs:
                    try:
                        flush_T()
                        next(deq_jobs[0])
                        break
                    except StopIteration:
                        deq_jobs.pop(0)
                else:
                    break
            if not deq_jobs and not deq_done[0]:
                flush_T()
                deq_done[0] = True
                fw.cc("pool", lambda e: e.collective_compute(
                    "AllGather", ALU.bypass, replica_groups=RG,
                    ins=[ag3g_in.opt()], outs=[ag3g_out.opt()]), [ag3g_in], [ag3g_out])
                fw.cc("pool", lambda e: e.collective_compute(
                    "AllGather", ALU.bypass, replica_groups=RG,
                    ins=[ag3d_in.opt()], outs=[ag3d_out.opt()]), [ag3d_in], [ag3d_out])

        for hh_ in range(HPC):
            for wsl, wsrc in ((wqkv[hh_][:, 0:2048], wq_in),
                              (wqkv[hh_][:, 2048:4096], wk_in),
                              (wqkv[hh_][:, 4096:6144], wv_in)):
                fw.dma("pool", wsl.rearrange("p (k e) -> p k e", k=16),
                       wsrc[hh_].rearrange("(k p) e -> p k e", p=128))
        for h in range(HPC):
            for tb in range(NTB):
                bb = tb % 2
                t0 = 512 * tb
                for k in range(16):
                    q_ = "sp" if k % 2 == 0 else "act"
                    seg = 0
                    while seg < 512:
                        blkc = (t0 + seg) // TLOC
                        off = (t0 + seg) % TLOC
                        w_ = min(512 - seg, TLOC - off)
                        fw.dma(q_, ln1t[bb][k][:, seg:seg + w_],
                               ag1_out[D * blkc + 128 * k:
                                       D * blkc + 128 * (k + 1),
                                       off:off + w_])
                        seg += w_
                for k in range(16):
                    fw.op("pe", lambda e, k=k, bb=bb, h=h: e.matmul(
                        ps_S[0], wqkv[h][:, 128 * k:128 * (k + 1)], ln1t[bb][k],
                        start=(k == 0), stop=(k == 15)),
                        reads=[wqkv[h], ln1t[bb][k]], writes=[ps_S[0]])
                    fw.op("pe", lambda e, k=k, bb=bb, h=h: e.matmul(
                        ps_S[1], wqkv[h][:, 2048 + 128 * k:2048 + 128 * (k + 1)],
                        ln1t[bb][k], start=(k == 0), stop=(k == 15)),
                        reads=[wqkv[h], ln1t[bb][k]], writes=[ps_S[1]])
                fw.op("dve", lambda e, tb=tb, h=h: e.tensor_scalar(
                    out=Qfm[:, 512 * tb:512 * (tb + 1)], in0=ps_S[0],
                    scalar1=c_isq, scalar2=bqs[:, h:h + 1],
                    op0=ALU.mult, op1=ALU.add),
                    reads=[ps_S[0], c_isq, bqs], writes=[Qfm])
                fw.op("dve", lambda e, tb=tb, h=h: e.tensor_scalar(
                    out=Kfm[:, 512 * tb:512 * (tb + 1)], in0=ps_S[1],
                    scalar1=bks[:, h:h + 1], scalar2=None, op0=ALU.add),
                    reads=[ps_S[1], bks], writes=[Kfm])
                for k in range(16):
                    for vh in range(2):
                        fw.op("pe", lambda e, k=k, bb=bb, h=h, vh=vh: e.matmul(
                            ps_S[vh][:, 0:256],
                            wqkv[h][:, 4096 + 128 * k:4096 + 128 * (k + 1)],
                            ln1t[bb][k][:, 256 * vh:256 * (vh + 1)],
                            start=(k == 0), stop=(k == 15)),
                            reads=[wqkv[h], ln1t[bb][k]], writes=[ps_S[vh]])
                for vh in range(2):
                    fw.op("dve", lambda e, h=h, vh=vh: e.tensor_scalar(
                        out=vfm_sb[:, 256 * vh:256 * (vh + 1)],
                        in0=ps_S[vh][:, 0:256], scalar1=bvs[:, h:h + 1],
                        scalar2=None, op0=ALU.add),
                        reads=[ps_S[vh], bvs], writes=[vfm_sb])
                for vt in range(4):
                    fw.op("pe", lambda e, vt=vt: e.transpose(
                        ps_DQ[vt % 2], vfm_sb[:, 128 * vt:128 * (vt + 1)],
                        ident), reads=[vfm_sb, ident], writes=[ps_DQ[vt % 2]])
                    fw.op("dve", lambda e, tb=tb, vt=vt: e.tensor_copy(
                        out=Vtm[:, (4 * tb + vt) * 128:(4 * tb + vt + 1) * 128],
                        in_=ps_DQ[vt % 2]),
                        reads=[ps_DQ[vt % 2]], writes=[Vtm])
                deq_step(6)

            if dbg and h == 0:
                fw.dma("pool", dbg_qkv[0:128, :], Qfm)
                fw.dma("pool", dbg_qkv[128:256, :], Kfm)
                fw.dma("pool", dbg_qkv[256:384, :], Vtm)
            for side, mc, m1 in ((Kfm, mkc, mk1), (Qfm, mqc, mq1)):
                fw.op("act", lambda e, side=side: e.activation(out=scr,
                      in_=side, func=AF.Square, bias=0.0, scale=1.0),
                      reads=[side], writes=[scr])
                for ch in range(S // 512):
                    fw.op("pe", lambda e, ch=ch: e.matmul(
                        ps_D, onescol, scr[:, 512 * ch:512 * (ch + 1)],
                        start=True, stop=True),
                        reads=[onescol, scr], writes=[ps_D])
                    fw.op("dve", lambda e, ch=ch, mc=mc: e.reduce_max(
                        out=mc[:, ch:ch + 1], in_=ps_D,
                        axis=mybir.AxisListType.X), reads=[ps_D], writes=[mc])
                fw.op("dve", lambda e, mc=mc, m1=m1: e.reduce_max(out=m1,
                      in_=mc[:, 0:S // 512],
                      axis=mybir.AxisListType.X), reads=[mc], writes=[m1])
            fw.op("act", lambda e: e.activation(out=lnmk, in_=mk1, func=AF.Ln,
                  bias=0.0, scale=1.0), reads=[mk1], writes=[lnmk])
            fw.op("act", lambda e: e.activation(out=nc0, in_=mq1, func=AF.Ln,
                  bias=0.0, scale=1.0), reads=[mq1], writes=[nc0])
            fw.op("dve", lambda e: e.tensor_tensor(out=nc0, in0=nc0, in1=lnmk,
                  op=ALU.add), reads=[nc0, lnmk], writes=[nc0])
            fw.op("act", lambda e: e.activation(out=nc0, in_=nc0, func=AF.Exp,
                  bias=0.0, scale=c_half[0:1, :]),
                  reads=[nc0, c_half], writes=[nc0])
            fw.op("dve", lambda e: e.tensor_scalar(out=nc0, in0=nc0,
                  scalar1=-1.0, scalar2=None, op0=ALU.mult),
                  reads=[nc0], writes=[nc0])
            fw.op("pe", lambda e: e.matmul(ps_BC[:, 0:1], ones_f, nc0,
                  start=True, stop=True), reads=[ones_f, nc0], writes=[ps_BC])
            fw.op("dve", lambda e: e.tensor_copy(out=c0col, in_=ps_BC[:, 0:1]),
                  reads=[ps_BC], writes=[c0col])

            STB = [ps_S[0], ps_S[1], ps_BC]
            for qb in range(S // 512):
                zb = qb % 2
                nkt = 4 * qb + 4
                for kp in range(nkt + 2):
                    if kp < nkt:
                        kt = kp
                        pp = kt % 3
                        w0 = 128 * max(0, kt - 4 * qb)
                        fw.op("pe", lambda e, qb=qb, kt=kt, pp=pp: e.matmul(
                            STB[pp], Kfm[:, 128 * kt:128 * (kt + 1)],
                            Qfm[:, 512 * qb:512 * (qb + 1)],
                            start=True, stop=True),
                            reads=[Kfm, Qfm], writes=[STB[pp]])
                        if kt >= 4 * qb:
                            fw.op("dve", lambda e, pp=pp, w0=w0:
                                  e.scalar_tensor_tensor(
                                      out=STB[pp][:, w0:w0 + 128],
                                      in0=STB[pp][:, w0:w0 + 128],
                                      scalar=1.0, in1=maskT_t,
                                      op0=ALU.mult, op1=ALU.add),
                                  reads=[STB[pp], maskT_t], writes=[STB[pp]])
                        fw.op("act", lambda e, pp=pp: e.activation(
                              out=PT_sb[pp], in_=STB[pp], func=AF.Exp,
                              bias=c0col, scale=1.0),
                              reads=[STB[pp], c0col], writes=[PT_sb[pp]])
                        if w0 > 0:
                            fw.op("dve", lambda e, pp=pp, w0=w0: e.memset(
                                PT_sb[pp][:, 0:w0], 0.0), writes=[PT_sb[pp]])
                    if kp >= 2:
                        kt = kp - 2
                        pp = kt % 3
                        fw.op("pe", lambda e, kt=kt, pp=pp, nkt=nkt: e.matmul(
                            ps_D, onescol, PT_sb[pp],
                            start=(kt == 0), stop=(kt == nkt - 1)),
                            reads=[onescol, PT_sb[pp]], writes=[ps_D])
                        fw.op("pe", lambda e, kt=kt, pp=pp, zb=zb, nkt=nkt:
                              e.matmul(
                              ps_Z[zb], Vtm[:, 128 * kt:128 * (kt + 1)],
                              PT_sb[pp],
                              start=(kt == 0), stop=(kt == nkt - 1)),
                              reads=[Vtm, PT_sb[pp]], writes=[ps_Z[zb]])
                fw.op("dve", lambda e: e.tensor_copy(out=drow, in_=ps_D),
                      reads=[ps_D], writes=[drow])
                fw.op("dve", lambda e: e.reciprocal(out=drec, in_=drow),
                      reads=[drow], writes=[drec])
                fw.op("pe", lambda e: e.matmul(ps_BC, ones_f, drec,
                      start=True, stop=True),
                      reads=[ones_f, drec], writes=[ps_BC])
                fw.op("dve", lambda e: e.tensor_copy(out=dinvb, in_=ps_BC),
                      reads=[ps_BC], writes=[dinvb])
                fw.op("dve", lambda e, zb=zb: e.tensor_tensor(out=z_sb[zb],
                      in0=ps_Z[zb], in1=dinvb, op=ALU.mult),
                      reads=[ps_Z[zb], dinvb], writes=[z_sb[zb]])
                if dbg and h == 0 and qb == 0:
                    fw.dma("pool", dbg_z0, z_sb[zb][:, 0:256])
                t0 = 512 * qb
                seg = 0
                while seg < 512:
                    jdst = (t0 + seg) // TLOC
                    off = (t0 + seg) % TLOC
                    w_ = min(512 - seg, TLOC - off)
                    fw.dma("act", a2a_in[HPC * DH * jdst + DH * h:
                                         HPC * DH * jdst + DH * (h + 1),
                                         off:off + w_],
                           z_sb[zb][:, seg:seg + w_])
                    seg += w_
                deq_step(10)

        while deq_jobs or not deq_done[0]:
            deq_step()

        def a2a_fn(e):
            with allow_ici_all_to_all():
                return e.collective_compute(
                    "AllToAll", ALU.bypass, replica_groups=RG,
                    ins=[a2a_in.opt()], outs=[a2a_out.opt()])
        fw.cc("pool", a2a_fn, [a2a_in], [a2a_out])
      fw.phase_barrier()

      phC = ExitStack()
      with phC:
        ps_O = [psum(phC, f"ps_O{i}", [128, 512], F32) for i in range(4)]
        wo_bf = sbuf(phC, "wo_bf", [128, NH * D], BF16)
        z_t = [sbuf(phC, f"z_t{i}", [128, NH * 128], BF16) for i in range(2)]
        xc = [sbuf(phC, f"xc{i}", [128, D], F32) for i in range(2)]
        att_sb = sbuf(phC, "att_sb", [128, D], F32)
        x1_sb = sbuf(phC, "x1_sb", [128, D], F32)
        ln2_sb = sbuf(phC, "ln2_sb", [128, D], BF16)
        bn6c = sbuf(phC, "bn6c", [128, 24], F32)
        bn2c = sbuf(phC, "bn2c", [128, 2], F32)
        cve = sbuf(phC, "cve", [128, 1], F32)
        crs = sbuf(phC, "crs", [128, 1], F32)
        cnm = sbuf(phC, "cnm", [128, 1], F32)
        ln2F = sbuf(phC, "ln2F", [128, D], BF16)
        ps_T2 = [psum(phC, f"ps_T2{i}", [128, 128], BF16) for i in range(2)]

        fw.dma("pool", wo_bf.rearrange("p (h d) -> p h d", h=NH),
               wo_in.rearrange("h p d -> p h d"))

        for tt in range(NTT):
            b = tt % 2
            fw.dma("sp", z_t[b].rearrange("p (hh t) -> p hh t", hh=NH),
                   a2a_out.rearrange("(hh p) t -> p hh t", p=128)[
                       :, :, 128 * tt:128 * (tt + 1)])
            fw.dma("act", xc[b], x_in[128 * tt:128 * (tt + 1), :])
            for hh in range(NH):
                for dc in range(4):
                    fw.op("pe", lambda e, hh=hh, dc=dc, b=b: e.matmul(
                        ps_O[dc], z_t[b][:, 128 * hh:128 * (hh + 1)],
                        wo_bf[:, D * hh + 512 * dc:D * hh + 512 * (dc + 1)],
                        start=(hh == 0), stop=(hh == NH - 1)),
                        reads=[z_t[b], wo_bf], writes=[ps_O[dc]])
            for dc in range(4):
                fw.op("dve", lambda e, dc=dc: e.tensor_tensor(
                    out=att_sb[:, 512 * dc:512 * (dc + 1)], in0=ps_O[dc],
                    in1=bo_bc[:, 512 * dc:512 * (dc + 1)], op=ALU.add),
                    reads=[ps_O[dc], bo_bc], writes=[att_sb])
            ln_stats(att_sb, c_eps5, bn6c, bn2c, cve, crs, cnm)
            fw.op("dve", lambda e: e.tensor_scalar(out=x1_sb, in0=att_sb,
                  scalar1=crs, scalar2=cnm, op0=ALU.mult, op1=ALU.add),
                  reads=[att_sb, crs, cnm], writes=[x1_sb])
            fw.op("dve", lambda e, b=b: e.tensor_tensor(out=x1_sb, in0=x1_sb,
                  in1=xc[b], op=ALU.add), reads=[x1_sb, xc[b]], writes=[x1_sb])
            fw.dma("sp", x1_park[128 * tt:128 * (tt + 1), :], x1_sb)
            ln_stats(x1_sb, c_eps5, bn6c, bn2c, cve, crs, cnm)
            fw.op("dve", lambda e: e.tensor_scalar(out=ln2_sb, in0=x1_sb,
                  scalar1=crs, scalar2=cnm, op0=ALU.mult, op1=ALU.add),
                  reads=[x1_sb, crs, cnm], writes=[ln2_sb])
            for k in range(16):
                fw.op("pe", lambda e, k=k: e.transpose(
                    ps_T2[k % 2], ln2_sb[:, 128 * k:128 * (k + 1)], ident),
                    reads=[ln2_sb, ident], writes=[ps_T2[k % 2]])
                if k % 2 == 0:
                    fw.op("dve", lambda e, k=k: e.tensor_copy(
                        out=ln2F[:, 128 * k:128 * (k + 1)], in_=ps_T2[k % 2]),
                        reads=[ps_T2[k % 2]], writes=[ln2F])
                else:
                    fw.op("act", lambda e, k=k: e.copy(
                        out=ln2F[:, 128 * k:128 * (k + 1)], in_=ps_T2[k % 2]),
                        reads=[ps_T2[k % 2]], writes=[ln2F])
            fw.dma("act", ln2_park.rearrange("(k p) t -> p k t", p=128)[
                :, :, 128 * tt:128 * (tt + 1)],
                   ln2F.rearrange("p (k t) -> p k t", k=16))
      fw.phase_barrier()

      phDa = ExitStack()
      with phDa:
        h_fm = sbuf(phDa, "h_fm", [128, 48 * TLOC], BF16)
        phD = ExitStack()
        with phD:
            ln2f = sbuf(phD, "ln2f", [128, 16 * TLOC], BF16)
            gtmp = sbuf(phD, "gtmp", [128, 6 * TLOC], BF16)
            wb = [sbuf(phD, f"wb{i}", [128, 16 * 704], BF16) for i in range(2)]
            ps_G = [psum(phD, f"ps_G{i}", [128, TW], F32) for i in range(8)]

            for k in range(16):
                fw.dma("sp" if k % 2 == 0 else "act",
                       ln2f[:, TLOC * k:TLOC * (k + 1)],
                       ln2_park[128 * k:128 * (k + 1), :])

            def load_wbatch(buf, sec, half):
                for k in range(16):
                    r0 = 256 * (k // 2) + 128 * (k % 2)
                    c0 = half * II + 704 * sec
                    fw.dma("sp" if k % 2 == 0 else "act",
                           wb[buf][:, 704 * k:704 * (k + 1)],
                           ag3g_out[r0:r0 + 128, c0:c0 + 704])

            load_wbatch(0, 0, 0)
            step = 0
            for sec in range(8):
                for half in range(2):
                    buf = step % 2
                    nb = step + 1
                    if nb < 16:
                        load_wbatch(nb % 2, nb // 2, nb % 2)
                    for kk in range(6):
                        mrows = 128 if kk < 5 else 64
                        po = 2 * (kk % 4)
                        for k in range(16):
                            for t5 in range(NT5):
                                fw.op("pe", lambda e, kk=kk, t5=t5, k=k,
                                      buf=buf, mrows=mrows, po=po: e.matmul(
                                          ps_G[po + t5][0:mrows, :],
                                          wb[buf][:, 704 * k + 128 * kk:
                                                  704 * k + 128 * kk + mrows],
                                          ln2f[:, TLOC * k + TW * t5:
                                               TLOC * k + TW * (t5 + 1)],
                                          start=(k == 0), stop=(k == 15)),
                                      reads=[wb[buf], ln2f],
                                      writes=[ps_G[po + t5]])
                        ch = sec * 6 + kk
                        for t5 in range(NT5):
                            if half == 0:
                                fw.op("act", lambda e, kk=kk, t5=t5, po=po,
                                      mrows=mrows: e.activation(
                                          out=gtmp[0:mrows,
                                                   TLOC * kk + TW * t5:
                                                   TLOC * kk + TW * (t5 + 1)],
                                          in_=ps_G[po + t5][0:mrows, :],
                                          func=AF.Silu, bias=0.0, scale=1.0),
                                      reads=[ps_G[po + t5]], writes=[gtmp])
                            else:
                                fw.op("dve", lambda e, ch=ch, kk=kk, t5=t5,
                                      po=po, mrows=mrows: e.tensor_tensor(
                                          out=h_fm[0:mrows,
                                                   TLOC * ch + TW * t5:
                                                   TLOC * ch + TW * (t5 + 1)],
                                          in0=gtmp[0:mrows,
                                                   TLOC * kk + TW * t5:
                                                   TLOC * kk + TW * (t5 + 1)],
                                          in1=ps_G[po + t5][0:mrows, :],
                                          op=ALU.mult),
                                      reads=[gtmp, ps_G[po + t5]],
                                      writes=[h_fm])
                                if mrows == 64:
                                    fw.op("dve", lambda e, ch=ch, t5=t5:
                                          e.memset(h_fm[64:128,
                                                   TLOC * ch + TW * t5:
                                                   TLOC * ch + TW * (t5 + 1)],
                                                   0.0), writes=[h_fm])
                    step += 1
        fw.phase_barrier()

        phD2 = ExitStack()
        with phD2:
            ps_Y = [psum(phD2, f"ps_Y{i}", [128, 512], F32) for i in range(8)]
            wd = [sbuf(phD2, f"wd{i}", [128, 512], BF16) for i in range(8)]
            yst = sbuf(phD2, "yst", [128, 512], F32)
            yv = [sbuf(phD2, f"yv{i}", [128, D], F32) for i in range(4)]
            vv = sbuf(phD2, "vv", [128, D], F32)
            x1r = sbuf(phD2, "x1r", [128, D], F32)
            o_sb = sbuf(phD2, "o_sb", [128, D], F32)
            bn6d = sbuf(phD2, "bn6d", [128, 24], F32)
            bn2d = sbuf(phD2, "bn2d", [128, 2], F32)
            dve_c = sbuf(phD2, "dve_c", [128, 1], F32)
            drs = sbuf(phD2, "drs", [128, 1], F32)
            dnm = sbuf(phD2, "dnm", [128, 1], F32)
            dm2 = sbuf(phD2, "dm2", [128, 1], F32)

            def down_mms(tp):
                for qd in range(4):
                    for ic in range(48):
                        buf = ic % 8
                        blk = ic // 6
                        ii = 128 * (ic % 6)
                        fw.dma("sp" if ic % 2 == 0 else "act", wd[buf],
                               ag3d_out[768 * blk + ii:768 * blk + ii + 128,
                                        512 * qd:512 * (qd + 1)])
                        for tti in range(2):
                            tt = 2 * tp + tti
                            fw.op("pe", lambda e, ic=ic, tt=tt, buf=buf,
                                  tti=tti, qd=qd: e.matmul(
                                  ps_Y[4 * tti + qd],
                                  h_fm[:, TLOC * ic + 128 * tt:
                                       TLOC * ic + 128 * (tt + 1)],
                                  wd[buf], start=(ic == 0), stop=(ic == 47)),
                                  reads=[h_fm, wd[buf]],
                                  writes=[ps_Y[4 * tti + qd]])
                for tti in range(2):
                    tt = 2 * tp + tti
                    for qd in range(4):
                        fw.op("dve", lambda e, tti=tti, qd=qd, tt=tt:
                              e.tensor_copy(
                              out=yv[tt % 4][:, 512 * qd:512 * (qd + 1)],
                              in_=ps_Y[4 * tti + qd]),
                              reads=[ps_Y[4 * tti + qd]], writes=[yv[tt % 4]])

            def epilogue(tt):
                yvt = yv[tt % 4]
                fw.dma("act", x1r, x1_park[128 * tt:128 * (tt + 1), :])
                for c in range(4):
                    fw.op("dve", lambda e, c=c, yvt=yvt: e.bn_stats(
                        out=bn6d[:, 6 * c:6 * c + 6],
                        in_=yvt[:, 512 * c:512 * (c + 1)]),
                        reads=[yvt], writes=[bn6d])
                fw.op("dve", lambda e: e.bn_aggr(out=bn2d, in_=bn6d),
                      reads=[bn6d], writes=[bn2d])
                fw.op("dve", lambda e: e.scalar_tensor_tensor(out=dm2,
                      in0=bn2d[:, 0:1], scalar=bn2d[:, 0:1], in1=bn2d[:, 1:2],
                      op0=ALU.mult, op1=ALU.add), reads=[bn2d], writes=[dm2])
                fw.op("act", lambda e: e.activation(out=dve_c, in_=dm2,
                      func=AF.Ln, bias=c_eps6, scale=1.0),
                      reads=[dm2, c_eps6], writes=[dve_c])
                fw.op("act", lambda e: e.activation(out=drs, in_=dve_c,
                      func=AF.Exp, bias=0.0, scale=c_nh),
                      reads=[dve_c, c_nh], writes=[drs])
                fw.op("dve", lambda e, yvt=yvt: e.tensor_scalar(out=vv, in0=yvt,
                      scalar1=drs, scalar2=None, op0=ALU.mult),
                      reads=[yvt, drs], writes=[vv])
                for c in range(4):
                    fw.op("dve", lambda e, c=c: e.bn_stats(
                        out=bn6d[:, 6 * c:6 * c + 6],
                        in_=vv[:, 512 * c:512 * (c + 1)]),
                        reads=[vv], writes=[bn6d])
                fw.op("dve", lambda e: e.bn_aggr(out=bn2d, in_=bn6d),
                      reads=[bn6d], writes=[bn2d])
                fw.op("act", lambda e: e.activation(out=dve_c, in_=bn2d[:, 1:2],
                      func=AF.Ln, bias=c_eps5, scale=1.0),
                      reads=[bn2d, c_eps5], writes=[dve_c])
                fw.op("act", lambda e: e.activation(out=drs, in_=dve_c,
                      func=AF.Exp, bias=0.0, scale=c_nh),
                      reads=[dve_c, c_nh], writes=[drs])
                fw.op("dve", lambda e: e.scalar_tensor_tensor(out=dnm,
                      in0=bn2d[:, 0:1], scalar=-1.0, in1=drs,
                      op0=ALU.mult, op1=ALU.mult),
                      reads=[bn2d, drs], writes=[dnm])
                fw.op("dve", lambda e: e.tensor_scalar(out=o_sb, in0=vv,
                      scalar1=drs, scalar2=dnm, op0=ALU.mult, op1=ALU.add),
                      reads=[vv, drs, dnm], writes=[o_sb])
                fw.op("dve", lambda e: e.tensor_tensor(out=o_sb, in0=o_sb,
                      in1=x1r, op=ALU.add), reads=[o_sb, x1r], writes=[o_sb])
                fw.dma("sp", out_t[128 * tt:128 * (tt + 1), :], o_sb)

            NPAIR = NTT // 2
            down_mms(0)
            for tp in range(1, NPAIR):
                down_mms(tp)
                epilogue(2 * (tp - 1))
                epilogue(2 * (tp - 1) + 1)
            epilogue(2 * (NPAIR - 1))
            epilogue(2 * (NPAIR - 1) + 1)

      fw.wait_all("sp")
      fw.wait_all("act")
      fw.wait_all("pool")

      fw.prune_incs()
      sems = {s: es.enter_context(nc.semaphore("fws_" + s))
              for s in sorted(fw.sem_names)}
      with nc.Block() as block:
          @block.sync
          def _(e):
              fw.emit_queue("sp", e, sems)

          @block.vector
          def _(e):
              fw.emit_queue("dve", e, sems)

          @block.scalar
          def _(e):
              fw.emit_queue("act", e, sems)

          @block.gpsimd
          def _(e):
              fw.emit_queue("pool", e, sems)

          @block.tensor
          def _(e):
              fw.emit_queue("pe", e, sems)
    return nc


def shard_inputs(inputs, S):
    TLOC = S // NC
    x = np.asarray(inputs["L0_attn_x"], dtype=np.float32).reshape(S, D)
    wq = np.asarray(inputs["L0_attn_W_Q"], dtype=np.float32)
    wk = np.asarray(inputs["L0_attn_W_K"], dtype=np.float32)
    wv = np.asarray(inputs["L0_attn_W_V"], dtype=np.float32)
    wo = np.ascontiguousarray(np.asarray(inputs["L0_attn_W_O"], dtype=np.float32))
    bq = np.asarray(inputs["L0_attn_b_Q"], dtype=np.float32)
    bk = np.asarray(inputs["L0_attn_b_K"], dtype=np.float32)
    bv = np.asarray(inputs["L0_attn_b_V"], dtype=np.float32)
    bo = np.ascontiguousarray(
        np.asarray(inputs["L0_attn_b_O"], dtype=np.float32).reshape(1, D))
    gup = np.asarray(inputs["L0_ffn_gate_up_packed"], dtype=np.int32)
    gus = np.asarray(inputs["L0_ffn_gate_up_scales"], dtype=np.float16)
    dnp = np.asarray(inputs["L0_ffn_down_packed"], dtype=np.int32)
    dns = np.asarray(inputs["L0_ffn_down_scales"], dtype=np.float16)

    maps = []
    for c in range(NC):
        m = {}
        m["x_in"] = np.ascontiguousarray(x[TLOC * c:TLOC * (c + 1)])
        m["wq_in"] = np.ascontiguousarray(wq[HPC * c:HPC * (c + 1)])
        m["wk_in"] = np.ascontiguousarray(wk[HPC * c:HPC * (c + 1)])
        m["wv_in"] = np.ascontiguousarray(wv[HPC * c:HPC * (c + 1)])
        m["wo_in"] = wo
        m["bq_in"] = np.ascontiguousarray(bq[HPC * c:HPC * (c + 1)])
        m["bk_in"] = np.ascontiguousarray(bk[HPC * c:HPC * (c + 1)])
        m["bv_in"] = np.ascontiguousarray(bv[HPC * c:HPC * (c + 1)])
        m["bo_in"] = bo
        g = gup[:, GU_W * c:GU_W * (c + 1)].reshape(88, 128, GU_W)
        m["gup_in"] = np.ascontiguousarray(g.transpose(1, 0, 2))
        sc = gus[2 * c:2 * c + 2, :]
        e = sc.reshape(2, 88, 128).transpose(2, 1, 0)
        m["gus_in"] = np.ascontiguousarray(np.repeat(e, 128, axis=2))
        dd = dnp[:, DN_W * c:DN_W * (c + 1)].reshape(16, 128, DN_W)
        m["dnp_in"] = np.ascontiguousarray(dd.transpose(1, 0, 2))
        il = np.minimum(np.arange(768), 703)
        gidx = (704 * c + il) // 128
        ee = dns[gidx, :].reshape(768, 16, 128).transpose(2, 1, 0)
        m["dns_in"] = np.ascontiguousarray(ee)
        maps.append(m)
    return maps


def kernel(**inputs):
    S = int(np.asarray(inputs["L0_attn_x"]).shape[1])
    in_maps = shard_inputs(inputs, S)
    nc = build_program(S)
    res = run_bass_kernel_spmd(nc, in_maps, core_ids=list(range(NC)))
    out = np.concatenate([res.results[c]["out"] for c in range(NC)], axis=0)
    return out.reshape(1, S, D).astype(np.float32)


if __name__ == "__main__":
    nc = build_program(2048)
    n = sum(len(v) for v in nc.__dict__.get("_noexist", {})) if False else None
    print("built ok")


# version 28
# speedup vs baseline: 1.1409x; 1.1409x over previous
import math
import numpy as np
import concourse.bass as bass
import concourse.mybir as mybir
from concourse.bass_utils import run_bass_kernel_spmd
from concourse.aot_env import allow_ici_all_to_all
from contextlib import ExitStack

F32 = mybir.dt.float32
BF16 = mybir.dt.bfloat16
F16 = mybir.dt.float16
I32 = mybir.dt.int32
AF = mybir.ActivationFunctionType
ALU = mybir.AluOpType

NC = 8
D = 2048
NH = 16
DH = 128
HPC = NH // NC
II = 5632
GU_W = 32
DN_W = 88

QUEUES = ("pe", "act", "dve", "pool", "sp")


def _th(x):
    t = getattr(x, "tensor", None)
    if t is None:
        t = x
    return t.name


class Op:
    __slots__ = ("q", "fn", "waits", "token", "inc")

    def __init__(self, q, fn, waits, token, inc):
        self.q, self.fn, self.waits, self.token, self.inc = q, fn, waits, token, inc


class Fw:
    N_DMA_LANES = 8

    def __init__(self):
        self.ops = {q: [] for q in QUEUES}
        self.counts = {}
        self.tstate = {}
        self.satisfied = {q: {} for q in QUEUES}
        self.sem_names = set()
        self.dma_rr = {q: 0 for q in QUEUES}
        self.cc_idx = 0

    def _sem_for(self, q, kind):
        if kind == "cc":
            name = f"cc{self.cc_idx}"
            self.cc_idx += 1
        elif kind == "dma":
            lane = self.dma_rr[q] % self.N_DMA_LANES
            self.dma_rr[q] += 1
            name = f"dma_{q}_{lane}"
        else:
            name = q
        self.sem_names.add(name)
        return name

    def _dep_tokens(self, reads, writes):
        deps = []
        for ap_ in reads:
            st = self.tstate.get(_th(ap_))
            if st and st[0]:
                deps.append(st[0])
        for ap_ in writes:
            st = self.tstate.get(_th(ap_))
            if st:
                if st[0]:
                    deps.append(st[0])
                deps.extend(st[1])
        return deps

    def _record(self, q, fn, reads, writes, kind, inc):
        deps = self._dep_tokens(reads, writes)
        sname = self._sem_for(q, kind)
        if kind != "compute":
            prev = self.counts.get(sname, 0)
            if prev:
                deps.append((sname, prev))
        sat = self.satisfied[q]
        waits = []
        best = {}
        for (s, v) in deps:
            if v > best.get(s, 0):
                best[s] = v
        for s, v in sorted(best.items()):
            if sat.get(s, 0) < v:
                waits.append((s, v))
                sat[s] = v
        self.counts[sname] = self.counts.get(sname, 0) + inc
        token = (sname, self.counts[sname])
        self.ops[q].append(Op(q, fn, waits, token, inc))
        for ap_ in reads:
            st = self.tstate.setdefault(_th(ap_), [None, []])
            st[1].append(token)
        for ap_ in writes:
            self.tstate[_th(ap_)] = [token, []]
        return token

    def op(self, q, fn, reads=(), writes=()):
        return self._record(q, fn, reads, writes, "compute", 1)

    def dma(self, q, out, in_, **kw):
        return self._record(q, lambda e: e.dma_start(out=out, in_=in_, **kw),
                            [in_], [out], "dma", 16)

    def dma_t(self, q, out, in_):
        return self._record(q, lambda e: e.dma_start(out=out, in_=in_,
                                                     transpose=True),
                            [in_], [out], "dma", 16)

    def cc(self, q, fn, ins, outs):
        return self._record(q, fn, list(ins), list(outs), "cc", 1)

    def wait_all(self, q):
        sat = self.satisfied[q]
        waits = []
        for s, v in sorted(self.counts.items()):
            if sat.get(s, 0) < v:
                waits.append((s, v))
                sat[s] = v
        if waits:
            self.ops[q].append(Op(q, None, waits, None, 0))

    def phase_barrier(self):
        for q in QUEUES:
            self.wait_all(q)

    def prune_incs(self):
        awaited = {}
        for q in QUEUES:
            for op_ in self.ops[q]:
                for (s, v) in op_.waits:
                    awaited.setdefault(s, set()).add(v)
        remap = {}
        keep = {}
        for s in list(self.counts):
            if s.startswith("dma_") or s.startswith("cc"):
                continue
            want = awaited.get(s, set())
            m = {}
            newc = 0
            kept = set()
            for old in range(1, self.counts[s] + 1):
                if old in want:
                    newc += 1
                    kept.add(old)
                m[old] = newc
            remap[s] = m
            keep[s] = kept
        for q in QUEUES:
            for op_ in self.ops[q]:
                op_.waits = [(s, remap[s][v] if s in remap else v)
                             for (s, v) in op_.waits]
                if op_.token is not None and op_.token[0] in remap:
                    s, v = op_.token
                    op_.inc = 1 if v in keep[s] else 0

    def emit_queue(self, q, eng, sems):
        for op_ in self.ops[q]:
            for (s, v) in op_.waits:
                eng.wait_ge(sems[s], v)
            if op_.fn is not None:
                ins_ = op_.fn(eng)
                if op_.token is None:
                    pass
                elif op_.token[0].startswith("cc"):
                    ins_.then_inc(sems[op_.token[0]])
                elif op_.inc:
                    ins_.then_inc(sems[op_.token[0]], op_.inc)


def build_program(S, dbg=False):
    TLOC = S // NC
    NTT = TLOC // 128
    NQB = S // 256
    NTB = S // 512
    TW = min(512, TLOC)
    NT5 = TLOC // TW
    ISQ = 1.0 / math.sqrt(DH)
    RG = [list(range(NC))]

    nc = bass.Bass("TRN2", target_bir_lowering=False, debug=False)

    ap = lambda t: t.ap()
    x_in = ap(nc.dram_tensor("x_in", [TLOC, D], F32, kind="ExternalInput"))
    wq_in = ap(nc.dram_tensor("wq_in", [HPC, D, DH], F32, kind="ExternalInput"))
    wk_in = ap(nc.dram_tensor("wk_in", [HPC, D, DH], F32, kind="ExternalInput"))
    wv_in = ap(nc.dram_tensor("wv_in", [HPC, D, DH], F32, kind="ExternalInput"))
    wo_in = ap(nc.dram_tensor("wo_in", [NH, DH, D], F32, kind="ExternalInput"))
    bq_in = ap(nc.dram_tensor("bq_in", [HPC, DH], F32, kind="ExternalInput"))
    bk_in = ap(nc.dram_tensor("bk_in", [HPC, DH], F32, kind="ExternalInput"))
    bv_in = ap(nc.dram_tensor("bv_in", [HPC, DH], F32, kind="ExternalInput"))
    bo_in = ap(nc.dram_tensor("bo_in", [1, D], F32, kind="ExternalInput"))
    gup_in = ap(nc.dram_tensor("gup_in", [128, 88, GU_W], I32, kind="ExternalInput"))
    gus_in = ap(nc.dram_tensor("gus_in", [128, 88, 256], F16, kind="ExternalInput"))
    dnp_in = ap(nc.dram_tensor("dnp_in", [128, 16, DN_W], I32, kind="ExternalInput"))
    dns_in = ap(nc.dram_tensor("dns_in", [128, 16, 768], F16, kind="ExternalInput"))
    out_t = ap(nc.dram_tensor("out", [TLOC, D], F32, kind="ExternalOutput"))

    ag1_in = ap(nc.dram_tensor("ag1_in", [D, TLOC], BF16))
    ag1_out = ap(nc.dram_tensor("ag1_out", [NC * D, TLOC], BF16))
    a2a_in = ap(nc.dram_tensor("a2a_in", [NC * HPC * DH, TLOC], BF16))
    a2a_out = ap(nc.dram_tensor("a2a_out", [NC * HPC * DH, TLOC], BF16))
    ag3g_in = ap(nc.dram_tensor("ag3g_in", [256, 2 * II], BF16))
    ag3g_out = ap(nc.dram_tensor("ag3g_out", [NC * 256, 2 * II], BF16))
    ag3d_in = ap(nc.dram_tensor("ag3d_in", [768, D], BF16))
    ag3d_out = ap(nc.dram_tensor("ag3d_out", [NC * 768, D], BF16))
    x1_park = ap(nc.dram_tensor("x1_park", [TLOC, D], F32))
    dbg_qkv = ap(nc.dram_tensor("dbg_qkv", [3 * 128, S], BF16))
    dbg_sp = ap(nc.dram_tensor("dbg_sp", [3 * 128, 512], F32))
    dbg_z0 = ap(nc.dram_tensor("dbg_z0", [128, 256], BF16))
    dbg_pq = ap(nc.dram_tensor("dbg_pq", [3 * 128, 512], BF16))
    ln2_park = ap(nc.dram_tensor("ln2_park", [D, TLOC], BF16))
    y_park = ap(nc.dram_tensor("y_park", [TLOC, D], F32))

    fw = Fw()
    es = ExitStack()

    def sbuf(scope, name, shape, dtype):
        return scope.enter_context(nc.sbuf_tensor(name, shape, dtype))[:]

    def psum(scope, name, shape, dtype=F32):
        return scope.enter_context(nc.psum_tensor(name, shape, dtype))[:]

    c_eps5 = sbuf(es, "c_eps5", [128, 1], F32)
    c_eps6 = sbuf(es, "c_eps6", [128, 1], F32)
    c_nh = sbuf(es, "c_nh", [128, 1], F32)
    ones_f = sbuf(es, "ones_f", [1, 128], F32)
    mask_t = sbuf(es, "mask_t", [128, 128], F32)
    maskT_t = sbuf(es, "maskT_t", [128, 128], F32)
    iota_r = sbuf(es, "iota_r", [128, 128], F32)
    rowi = sbuf(es, "rowi", [128, 1], F32)
    bo_bc = sbuf(es, "bo_bc", [128, D], F32)
    bqs = sbuf(es, "bqs", [128, HPC], F32)
    bks = sbuf(es, "bks", [128, HPC], F32)
    bqr = sbuf(es, "bqr", [128, HPC], F32)
    bkr = sbuf(es, "bkr", [128, HPC], F32)
    bvs = sbuf(es, "bvs", [128, HPC], F32)
    c_isq = sbuf(es, "c_isq", [128, 1], F32)
    ident = sbuf(es, "ident", [128, 128], BF16)

    fw.op("dve", lambda e: e.memset(c_eps5, 1e-5), writes=[c_eps5])
    fw.op("dve", lambda e: e.memset(c_eps6, 1e-6), writes=[c_eps6])
    fw.op("dve", lambda e: e.memset(c_nh, -0.5), writes=[c_nh])
    fw.op("dve", lambda e: e.memset(c_isq, ISQ), writes=[c_isq])
    fw.op("dve", lambda e: e.memset(ones_f, 1.0), writes=[ones_f])
    fw.op("pool", lambda e: e.iota(iota_r, pattern=[[1, 128]], base=0,
          channel_multiplier=0, allow_small_or_imprecise_dtypes=True),
          writes=[iota_r])
    fw.op("pool", lambda e: e.iota(rowi, pattern=[[0, 1]], base=0,
          channel_multiplier=1, allow_small_or_imprecise_dtypes=True),
          writes=[rowi])
    fw.op("dve", lambda e: e.tensor_scalar(out=mask_t, in0=iota_r,
          scalar1=rowi, scalar2=-100000.0, op0=ALU.is_gt, op1=ALU.mult),
          reads=[iota_r, rowi], writes=[mask_t])
    fw.op("dve", lambda e: e.tensor_scalar(out=maskT_t, in0=iota_r,
          scalar1=rowi, scalar2=-100000.0, op0=ALU.is_lt, op1=ALU.mult),
          reads=[iota_r, rowi], writes=[maskT_t])
    fw.op("dve", lambda e: e.tensor_scalar(out=ident, in0=iota_r,
          scalar1=rowi, scalar2=None, op0=ALU.is_equal),
          reads=[iota_r, rowi], writes=[ident])

    fw.dma("sp", bqr, bq_in.rearrange("h e -> e h"),
           allow_slow_non_contiguous=True)
    fw.dma("sp", bkr, bk_in.rearrange("h e -> e h"),
           allow_slow_non_contiguous=True)
    fw.dma("sp", bvs, bv_in.rearrange("h e -> e h"),
           allow_slow_non_contiguous=True)
    fw.op("dve", lambda e: e.tensor_scalar(out=bqs, in0=bqr, scalar1=ISQ,
          scalar2=None, op0=ALU.mult), reads=[bqr], writes=[bqs])
    fw.op("dve", lambda e: e.tensor_copy(out=bks, in_=bkr),
          reads=[bkr], writes=[bks])

    with es:
      ph0 = ExitStack()
      with ph0:
        ps_a = psum(ph0, "ps_a", [128, 512], F32)
        ps_b = psum(ph0, "ps_b", [128, 512], F32)
        bo_row = sbuf(ph0, "bo_row", [1, D], F32)
        fw.dma("sp", bo_row, bo_in)
        for c4 in range(4):
            fw.op("pe", lambda e, c4=c4: e.matmul(
                ps_a[:, 0:512], ones_f, bo_row[:, 512 * c4:512 * (c4 + 1)],
                start=True, stop=True), reads=[ones_f, bo_row], writes=[ps_a])
            fw.op("dve", lambda e, c4=c4: e.tensor_copy(
                out=bo_bc[:, 512 * c4:512 * (c4 + 1)], in_=ps_a[:, 0:512]),
                reads=[ps_a], writes=[bo_bc])


        xa = [sbuf(ph0, f"xa{i}", [128, D], F32) for i in range(2)]
        lnt = [sbuf(ph0, f"lnt{i}", [128, D], BF16) for i in range(2)]
        lnF = [sbuf(ph0, f"lnF{i}", [128, D], BF16) for i in range(2)]
        ps_T = [psum(ph0, f"ps_T{i}", [128, 128], BF16) for i in range(2)]
        bn6 = sbuf(ph0, "bn6", [128, 24], F32)
        bn2 = sbuf(ph0, "bn2", [128, 2], F32)
        st_ve = sbuf(ph0, "st_ve", [128, 1], F32)
        st_rs = sbuf(ph0, "st_rs", [128, 1], F32)
        st_nm = sbuf(ph0, "st_nm", [128, 1], F32)

        def ln_stats(src, eps_tile, b6, b2, ve, rs_, nm):
            for c in range(4):
                fw.op("dve", lambda e, c=c: e.bn_stats(
                    out=b6[:, 6 * c:6 * c + 6],
                    in_=src[:, 512 * c:512 * (c + 1)]), reads=[src], writes=[b6])
            fw.op("dve", lambda e: e.bn_aggr(out=b2, in_=b6),
                  reads=[b6], writes=[b2])
            fw.op("act", lambda e: e.activation(out=ve, in_=b2[:, 1:2],
                  func=AF.Ln, bias=eps_tile, scale=1.0),
                  reads=[b2, eps_tile], writes=[ve])
            fw.op("act", lambda e: e.activation(out=rs_, in_=ve, func=AF.Exp,
                  bias=0.0, scale=c_nh), reads=[ve, c_nh], writes=[rs_])
            fw.op("dve", lambda e: e.scalar_tensor_tensor(out=nm,
                  in0=b2[:, 0:1], scalar=-1.0, in1=rs_,
                  op0=ALU.mult, op1=ALU.mult), reads=[b2, rs_], writes=[nm])

        for tt in range(NTT):
            b = tt % 2
            fw.dma("sp", xa[b], x_in[128 * tt:128 * (tt + 1), :])
            ln_stats(xa[b], c_eps5, bn6, bn2, st_ve, st_rs, st_nm)
            fw.op("dve", lambda e, b=b: e.tensor_scalar(out=lnt[b], in0=xa[b],
                  scalar1=st_rs, scalar2=st_nm, op0=ALU.mult, op1=ALU.add),
                  reads=[xa[b], st_rs, st_nm], writes=[lnt[b]])
            for k in range(16):
                fw.op("pe", lambda e, b=b, k=k: e.transpose(
                    ps_T[k % 2], lnt[b][:, 128 * k:128 * (k + 1)], ident),
                    reads=[lnt[b], ident], writes=[ps_T[k % 2]])
                eng = ("dve", "act")[k % 2]
                if eng == "dve":
                    fw.op("dve", lambda e, b=b, k=k: e.tensor_copy(
                        out=lnF[b][:, 128 * k:128 * (k + 1)], in_=ps_T[k % 2]),
                        reads=[ps_T[k % 2]], writes=[lnF[b]])
                else:
                    fw.op("act", lambda e, b=b, k=k: e.copy(
                        out=lnF[b][:, 128 * k:128 * (k + 1)], in_=ps_T[k % 2]),
                        reads=[ps_T[k % 2]], writes=[lnF[b]])
            fw.dma("act", ag1_in.rearrange("(k p) t -> p k t", p=128)[
                :, :, 128 * tt:128 * (tt + 1)],
                   lnF[b].rearrange("p (k t) -> p k t", k=16))

        fw.cc("pool", lambda e: e.collective_compute(
            "AllGather", ALU.bypass, replica_groups=RG,
            ins=[ag1_in.opt()], outs=[ag1_out.opt()]), [ag1_in], [ag1_out])
      fw.phase_barrier()

      phB = ExitStack()
      with phB:
        scr = sbuf(phB, "scr", [128, S], BF16)
        Qfm = sbuf(phB, "Qfm", [128, S], BF16)
        Kfm = sbuf(phB, "Kfm", [128, S], BF16)
        onescol = sbuf(phB, "onescol", [128, 1], BF16)
        vfm_sb = sbuf(phB, "vfm_sb", [128, 512], BF16)
        Vtm = sbuf(phB, "Vtm", [128, S], BF16)
        PT_sb = [sbuf(phB, f"PT{i}", [128, 512], BF16) for i in range(3)]
        z_sb = [sbuf(phB, f"zsb{i}", [128, 512], BF16) for i in range(2)]
        mkc = sbuf(phB, "mkc", [1, 16], F32)
        mqc = sbuf(phB, "mqc", [1, 16], F32)
        mk1 = sbuf(phB, "mk1", [1, 1], F32)
        mq1 = sbuf(phB, "mq1", [1, 1], F32)
        lnmk = sbuf(phB, "lnmk", [1, 1], F32)
        nc0 = sbuf(phB, "nc0", [1, 1], F32)
        c0col = sbuf(phB, "c0col", [128, 1], F32)
        drow = sbuf(phB, "drow", [1, 512], F32)
        drec = sbuf(phB, "drec", [1, 512], F32)
        dinvb = sbuf(phB, "dinvb", [128, 512], F32)
        zsum_sb = sbuf(phB, "zsum_sb", [128, 512], F32)
        c_half = sbuf(phB, "c_half", [128, 1], F32)
        wqkv = [sbuf(phB, f"wqkv{i}", [128, 3 * 16 * 128], BF16) for i in range(2)]
        ln1t = [[sbuf(phB, f"ln1t{bb}_{k}", [128, 512], BF16) for k in range(16)]
                for bb in range(2)]
        ps_S = [psum(phB, f"ps_S{i}", [128, 512], F32) for i in range(2)]
        ps_ST = ps_S
        ps_Z = [psum(phB, f"ps_Z{i}", [128, 512], F32) for i in range(2)]
        ps_D = psum(phB, "ps_D", [1, 512], F32)
        ps_BC = psum(phB, "ps_BC", [128, 512], F32)
        ps_DQ = [psum(phB, f"ps_DQ{i}", [128, 128], BF16) for i in range(2)]

        dbg_p_sb = sbuf(phB, "dbg_p_sb", [128, 512], BF16)
        GNT = 8
        DNT = 2
        dq_pk = sbuf(phB, "dq_pk", [128, GNT * GU_W], I32)
        dq_sc = sbuf(phB, "dq_sc", [128, GNT * 256], F16)
        dq_st = sbuf(phB, "dq_st", [128, GNT * 256], BF16)
        dq_t0 = sbuf(phB, "dq_t0", [128, GNT * GU_W], I32)
        dq_t1 = sbuf(phB, "dq_t1", [128, GNT * GU_W], I32)
        dq_t2 = sbuf(phB, "dq_t2", [128, GNT * GU_W], I32)
        dq_t3 = sbuf(phB, "dq_t3", [128, GNT * GU_W], I32)
        dqT_st = sbuf(phB, "dqT_st", [128, 2 * GNT * 128], BF16)
        dnT_st = sbuf(phB, "dnT_st", [128, 6 * DNT * 128], BF16)
        dn_pk = sbuf(phB, "dn_pk", [128, DNT * DN_W], I32)
        dn_sc = sbuf(phB, "dn_sc", [128, DNT * 768], F16)
        dn_st = sbuf(phB, "dn_st", [128, DNT * 768], BF16)

        def bitchain(src_, n_el, j):
            fw.op("dve", lambda e: e.tensor_scalar(out=dq_t0[:, 0:n_el],
                  in0=src_, scalar1=4 * j, scalar2=15,
                  op0=ALU.logical_shift_right, op1=ALU.bitwise_and),
                  reads=[src_], writes=[dq_t0])
            fw.op("dve", lambda e: e.tensor_scalar(out=dq_t3[:, 0:n_el],
                  in0=dq_t0[:, 0:n_el], scalar1=7, scalar2=None,
                  op0=ALU.bitwise_and),
                  reads=[dq_t0], writes=[dq_t3])
            fw.op("dve", lambda e: e.tensor_scalar(out=dq_t1[:, 0:n_el],
                  in0=dq_t3[:, 0:n_el], scalar1=2, scalar2=None,
                  op0=ALU.is_ge), reads=[dq_t3], writes=[dq_t1])
            fw.op("dve", lambda e: e.tensor_tensor(out=dq_t1[:, 0:n_el],
                  in0=dq_t3[:, 0:n_el], in1=dq_t1[:, 0:n_el], op=ALU.add),
                  reads=[dq_t3, dq_t1], writes=[dq_t1])
            fw.op("dve", lambda e: e.tensor_scalar(out=dq_t2[:, 0:n_el],
                  in0=dq_t3[:, 0:n_el], scalar1=1, scalar2=None,
                  op0=ALU.is_ge), reads=[dq_t3], writes=[dq_t2])
            fw.op("dve", lambda e: e.scalar_tensor_tensor(out=dq_t1[:, 0:n_el],
                  in0=dq_t1[:, 0:n_el], scalar=251, in1=dq_t2[:, 0:n_el],
                  op0=ALU.add, op1=ALU.mult),
                  reads=[dq_t1, dq_t2], writes=[dq_t1])
            fw.op("dve", lambda e: e.tensor_scalar(out=dq_t1[:, 0:n_el],
                  in0=dq_t1[:, 0:n_el], scalar1=22, scalar2=None,
                  op0=ALU.logical_shift_left), reads=[dq_t1], writes=[dq_t1])
            fw.op("dve", lambda e: e.tensor_scalar(out=dq_t2[:, 0:n_el],
                  in0=dq_t0[:, 0:n_el], scalar1=3, scalar2=31,
                  op0=ALU.logical_shift_right, op1=ALU.logical_shift_left),
                  reads=[dq_t0], writes=[dq_t2])
            fw.op("dve", lambda e: e.tensor_tensor(out=dq_t1[:, 0:n_el],
                  in0=dq_t1[:, 0:n_el], in1=dq_t2[:, 0:n_el],
                  op=ALU.bitwise_or), reads=[dq_t1, dq_t2], writes=[dq_t1])

        pending_T = []

        def gu_chunk(ci):
            nt = GNT
            fw.dma("pool", dq_pk, gup_in[:, nt * ci:nt * (ci + 1), :]
                   .rearrange("p t w -> p (t w)"))
            fw.dma("pool", dq_sc, gus_in[:, nt * ci:nt * (ci + 1), :]
                   .rearrange("p t d -> p (t d)"))
            yield
            for j in range(8):
                bitchain(dq_pk, nt * GU_W, j)
                st3 = dq_st.rearrange("p (t d) -> p t d", t=nt)[:, :, j::8]
                sc3 = dq_sc.rearrange("p (t d) -> p t d", t=nt)[:, :, j::8]
                b3 = dq_t1[:, 0:nt * GU_W].rearrange("p (t w) -> p t w", t=nt)
                fw.op("pool", lambda e, st3=st3, sc3=sc3, b3=b3: e.tensor_tensor(
                      out=st3, in0=b3.bitcast(F32), in1=sc3, op=ALU.mult),
                      reads=[dq_t1, dq_sc], writes=[dq_st])
                yield

            def emit_T(ci=ci, nt=nt):
                for t_ in range(nt):
                    for dc in range(2):
                        fw.op("pe", lambda e, t_=t_, dc=dc: e.transpose(
                            ps_DQ[(t_ * 2 + dc) % 2],
                            dq_st.rearrange("p (t d) -> p t d", t=nt)[
                                :, t_, 128 * dc:128 * (dc + 1)], ident),
                            reads=[dq_st, ident],
                            writes=[ps_DQ[(t_ * 2 + dc) % 2]])
                        fw.op("act", lambda e, t_=t_, dc=dc: e.copy(
                            out=dqT_st.rearrange("p (dc t o) -> p dc t o",
                                                 dc=2, t=nt)[:, dc, t_, :],
                            in_=ps_DQ[(t_ * 2 + dc) % 2]),
                            reads=[ps_DQ[(t_ * 2 + dc) % 2]], writes=[dqT_st])
                fw.dma("pool", ag3g_in.rearrange("(dc p) o -> p dc o", p=128)[
                    :, :, nt * 128 * ci:nt * 128 * (ci + 1)]
                    .rearrange("p dc (t o) -> p dc t o", t=nt),
                       dqT_st.rearrange("p (dc t o) -> p dc t o", dc=2, t=nt))
            pending_T.append(emit_T)

        def dn_chunk(ci):
            nt = DNT
            fw.dma("pool", dn_pk, dnp_in[:, nt * ci:nt * (ci + 1), :]
                   .rearrange("p t w -> p (t w)"))
            fw.dma("pool", dn_sc, dns_in[:, nt * ci:nt * (ci + 1), :]
                   .rearrange("p t d -> p (t d)"))
            fw.op("pool", lambda e: e.memset(dn_st, 0.0), writes=[dn_st])
            yield
            for j in range(8):
                bitchain(dn_pk, nt * DN_W, j)
                st3 = dn_st.rearrange("p (t d) -> p t d", t=nt)[:, :, 0:704][:, :, j::8]
                sc3 = dn_sc.rearrange("p (t d) -> p t d", t=nt)[:, :, 0:704][:, :, j::8]
                b3 = dq_t1[:, 0:nt * DN_W].rearrange("p (t w) -> p t w", t=nt)
                fw.op("pool", lambda e, st3=st3, sc3=sc3, b3=b3: e.tensor_tensor(
                      out=st3, in0=b3.bitcast(F32), in1=sc3, op=ALU.mult),
                      reads=[dq_t1, dn_sc], writes=[dn_st])
                yield

            def emit_T(ci=ci, nt=nt):
                for t_ in range(nt):
                    for ib in range(6):
                        fw.op("pe", lambda e, t_=t_, ib=ib: e.transpose(
                            ps_DQ[(t_ * 6 + ib) % 2],
                            dn_st.rearrange("p (t d) -> p t d", t=nt)[
                                :, t_, 128 * ib:128 * (ib + 1)], ident),
                            reads=[dn_st, ident],
                            writes=[ps_DQ[(t_ * 6 + ib) % 2]])
                        fw.op("act", lambda e, t_=t_, ib=ib: e.copy(
                            out=dnT_st.rearrange("p (ib t o) -> p ib t o",
                                                 ib=6, t=nt)[:, ib, t_, :],
                            in_=ps_DQ[(t_ * 6 + ib) % 2]),
                            reads=[ps_DQ[(t_ * 6 + ib) % 2]], writes=[dnT_st])
                fw.dma("pool", ag3d_in.rearrange("(ib p) o -> p ib o", p=128)[
                    :, :, nt * 128 * ci:nt * 128 * (ci + 1)]
                    .rearrange("p ib (t o) -> p ib t o", t=nt),
                       dnT_st.rearrange("p (ib t o) -> p ib t o", ib=6, t=nt))
            pending_T.append(emit_T)

        fw.op("dve", lambda e: e.memset(onescol, 1.0), writes=[onescol])
        fw.op("dve", lambda e: e.memset(c_half, 0.5), writes=[c_half])
        deq_jobs = [gu_chunk(ci) for ci in range(88 // GNT)]
        deq_jobs += [dn_chunk(ci) for ci in range(16 // DNT)]
        deq_done = [False]

        def flush_T():
            while pending_T:
                pending_T.pop(0)()

        def deq_step(n=1):
            for _ in range(n):
                while deq_jobs:
                    try:
                        flush_T()
                        next(deq_jobs[0])
                        break
                    except StopIteration:
                        deq_jobs.pop(0)
                else:
                    break
            if not deq_jobs and not deq_done[0]:
                flush_T()
                deq_done[0] = True
                fw.cc("pool", lambda e: e.collective_compute(
                    "AllGather", ALU.bypass, replica_groups=RG,
                    ins=[ag3g_in.opt()], outs=[ag3g_out.opt()]), [ag3g_in], [ag3g_out])
                fw.cc("pool", lambda e: e.collective_compute(
                    "AllGather", ALU.bypass, replica_groups=RG,
                    ins=[ag3d_in.opt()], outs=[ag3d_out.opt()]), [ag3d_in], [ag3d_out])

        for hh_ in range(HPC):
            for wsl, wsrc in ((wqkv[hh_][:, 0:2048], wq_in),
                              (wqkv[hh_][:, 2048:4096], wk_in),
                              (wqkv[hh_][:, 4096:6144], wv_in)):
                fw.dma("pool", wsl.rearrange("p (k e) -> p k e", k=16),
                       wsrc[hh_].rearrange("(k p) e -> p k e", p=128))
        for h in range(HPC):
            for tb in range(NTB):
                bb = tb % 2
                t0 = 512 * tb
                for k in range(16):
                    q_ = "sp" if k % 2 == 0 else "act"
                    seg = 0
                    while seg < 512:
                        blkc = (t0 + seg) // TLOC
                        off = (t0 + seg) % TLOC
                        w_ = min(512 - seg, TLOC - off)
                        fw.dma(q_, ln1t[bb][k][:, seg:seg + w_],
                               ag1_out[D * blkc + 128 * k:
                                       D * blkc + 128 * (k + 1),
                                       off:off + w_])
                        seg += w_
                for k in range(16):
                    fw.op("pe", lambda e, k=k, bb=bb, h=h: e.matmul(
                        ps_S[0], wqkv[h][:, 128 * k:128 * (k + 1)], ln1t[bb][k],
                        start=(k == 0), stop=(k == 15)),
                        reads=[wqkv[h], ln1t[bb][k]], writes=[ps_S[0]])
                    fw.op("pe", lambda e, k=k, bb=bb, h=h: e.matmul(
                        ps_S[1], wqkv[h][:, 2048 + 128 * k:2048 + 128 * (k + 1)],
                        ln1t[bb][k], start=(k == 0), stop=(k == 15)),
                        reads=[wqkv[h], ln1t[bb][k]], writes=[ps_S[1]])
                fw.op("dve", lambda e, tb=tb, h=h: e.tensor_scalar(
                    out=Qfm[:, 512 * tb:512 * (tb + 1)], in0=ps_S[0],
                    scalar1=c_isq, scalar2=bqs[:, h:h + 1],
                    op0=ALU.mult, op1=ALU.add),
                    reads=[ps_S[0], c_isq, bqs], writes=[Qfm])
                fw.op("dve", lambda e, tb=tb, h=h: e.tensor_scalar(
                    out=Kfm[:, 512 * tb:512 * (tb + 1)], in0=ps_S[1],
                    scalar1=bks[:, h:h + 1], scalar2=None, op0=ALU.add),
                    reads=[ps_S[1], bks], writes=[Kfm])
                for k in range(16):
                    for vh in range(2):
                        fw.op("pe", lambda e, k=k, bb=bb, h=h, vh=vh: e.matmul(
                            ps_S[vh][:, 0:256],
                            wqkv[h][:, 4096 + 128 * k:4096 + 128 * (k + 1)],
                            ln1t[bb][k][:, 256 * vh:256 * (vh + 1)],
                            start=(k == 0), stop=(k == 15)),
                            reads=[wqkv[h], ln1t[bb][k]], writes=[ps_S[vh]])
                for vh in range(2):
                    fw.op("dve", lambda e, h=h, vh=vh: e.tensor_scalar(
                        out=vfm_sb[:, 256 * vh:256 * (vh + 1)],
                        in0=ps_S[vh][:, 0:256], scalar1=bvs[:, h:h + 1],
                        scalar2=None, op0=ALU.add),
                        reads=[ps_S[vh], bvs], writes=[vfm_sb])
                for vt in range(4):
                    fw.op("pe", lambda e, vt=vt: e.transpose(
                        ps_DQ[vt % 2], vfm_sb[:, 128 * vt:128 * (vt + 1)],
                        ident), reads=[vfm_sb, ident], writes=[ps_DQ[vt % 2]])
                    fw.op("act", lambda e, tb=tb, vt=vt: e.copy(
                        out=Vtm[:, (4 * tb + vt) * 128:(4 * tb + vt + 1) * 128],
                        in_=ps_DQ[vt % 2]),
                        reads=[ps_DQ[vt % 2]], writes=[Vtm])
                deq_step(6)

            if dbg and h == 0:
                fw.dma("pool", dbg_qkv[0:128, :], Qfm)
                fw.dma("pool", dbg_qkv[128:256, :], Kfm)
                fw.dma("pool", dbg_qkv[256:384, :], Vtm)
            for side, mc, m1 in ((Kfm, mkc, mk1), (Qfm, mqc, mq1)):
                fw.op("act", lambda e, side=side: e.activation(out=scr,
                      in_=side, func=AF.Square, bias=0.0, scale=1.0),
                      reads=[side], writes=[scr])
                cprows = [ps_D, ps_Z[0][0:1, :], ps_Z[1][0:1, :],
                          ps_BC[0:1, :]]
                cptens = [ps_D, ps_Z[0], ps_Z[1], ps_BC]
                for ch in range(S // 512):
                    fw.op("pe", lambda e, ch=ch: e.matmul(
                        cprows[ch % 4], onescol,
                        scr[:, 512 * ch:512 * (ch + 1)],
                        start=True, stop=True),
                        reads=[onescol, scr], writes=[cptens[ch % 4]])
                    fw.op("dve", lambda e, ch=ch, mc=mc: e.reduce_max(
                        out=mc[:, ch:ch + 1], in_=cprows[ch % 4],
                        axis=mybir.AxisListType.X),
                        reads=[cptens[ch % 4]], writes=[mc])
                fw.op("dve", lambda e, mc=mc, m1=m1: e.reduce_max(out=m1,
                      in_=mc[:, 0:S // 512],
                      axis=mybir.AxisListType.X), reads=[mc], writes=[m1])
            fw.op("act", lambda e: e.activation(out=lnmk, in_=mk1, func=AF.Ln,
                  bias=0.0, scale=1.0), reads=[mk1], writes=[lnmk])
            fw.op("act", lambda e: e.activation(out=nc0, in_=mq1, func=AF.Ln,
                  bias=0.0, scale=1.0), reads=[mq1], writes=[nc0])
            fw.op("dve", lambda e: e.tensor_tensor(out=nc0, in0=nc0, in1=lnmk,
                  op=ALU.add), reads=[nc0, lnmk], writes=[nc0])
            fw.op("act", lambda e: e.activation(out=nc0, in_=nc0, func=AF.Exp,
                  bias=0.0, scale=c_half[0:1, :]),
                  reads=[nc0, c_half], writes=[nc0])
            fw.op("dve", lambda e: e.tensor_scalar(out=nc0, in0=nc0,
                  scalar1=-1.0, scalar2=None, op0=ALU.mult),
                  reads=[nc0], writes=[nc0])
            fw.op("pe", lambda e: e.matmul(ps_BC[:, 0:1], ones_f, nc0,
                  start=True, stop=True), reads=[ones_f, nc0], writes=[ps_BC])
            fw.op("dve", lambda e: e.tensor_copy(out=c0col, in_=ps_BC[:, 0:1]),
                  reads=[ps_BC], writes=[c0col])

            STB = [ps_S[0], ps_S[1], ps_BC]
            for qb in range(S // 512):
                zb = qb % 2
                nkt = 4 * qb + 4
                for kp in range(nkt + 2):
                    if kp < nkt:
                        kt = kp
                        pp = kt % 3
                        w0 = 128 * max(0, kt - 4 * qb)
                        fw.op("pe", lambda e, qb=qb, kt=kt, pp=pp: e.matmul(
                            STB[pp], Kfm[:, 128 * kt:128 * (kt + 1)],
                            Qfm[:, 512 * qb:512 * (qb + 1)],
                            start=True, stop=True),
                            reads=[Kfm, Qfm], writes=[STB[pp]])
                        if kt >= 4 * qb:
                            fw.op("dve", lambda e, pp=pp, w0=w0:
                                  e.scalar_tensor_tensor(
                                      out=STB[pp][:, w0:w0 + 128],
                                      in0=STB[pp][:, w0:w0 + 128],
                                      scalar=1.0, in1=maskT_t,
                                      op0=ALU.mult, op1=ALU.add),
                                  reads=[STB[pp], maskT_t], writes=[STB[pp]])
                        fw.op("act", lambda e, pp=pp: e.activation(
                              out=PT_sb[pp], in_=STB[pp], func=AF.Exp,
                              bias=c0col, scale=1.0),
                              reads=[STB[pp], c0col], writes=[PT_sb[pp]])
                        if w0 > 0:
                            fw.op("dve", lambda e, pp=pp, w0=w0: e.memset(
                                PT_sb[pp][:, 0:w0], 0.0), writes=[PT_sb[pp]])
                    if kp >= 2:
                        kt = kp - 2
                        pp = kt % 3
                        fw.op("pe", lambda e, kt=kt, pp=pp, nkt=nkt: e.matmul(
                            ps_D, onescol, PT_sb[pp],
                            start=(kt == 0), stop=(kt == nkt - 1)),
                            reads=[onescol, PT_sb[pp]], writes=[ps_D])
                        fw.op("pe", lambda e, kt=kt, pp=pp, zb=zb, nkt=nkt:
                              e.matmul(
                              ps_Z[zb], Vtm[:, 128 * kt:128 * (kt + 1)],
                              PT_sb[pp],
                              start=(kt == 0), stop=(kt == nkt - 1)),
                              reads=[Vtm, PT_sb[pp]], writes=[ps_Z[zb]])
                fw.op("dve", lambda e: e.tensor_copy(out=drow, in_=ps_D),
                      reads=[ps_D], writes=[drow])
                fw.op("dve", lambda e: e.reciprocal(out=drec, in_=drow),
                      reads=[drow], writes=[drec])
                fw.op("pe", lambda e: e.matmul(ps_BC, ones_f, drec,
                      start=True, stop=True),
                      reads=[ones_f, drec], writes=[ps_BC])
                fw.op("dve", lambda e: e.tensor_copy(out=dinvb, in_=ps_BC),
                      reads=[ps_BC], writes=[dinvb])
                fw.op("dve", lambda e, zb=zb: e.tensor_tensor(out=z_sb[zb],
                      in0=ps_Z[zb], in1=dinvb, op=ALU.mult),
                      reads=[ps_Z[zb], dinvb], writes=[z_sb[zb]])
                if dbg and h == 0 and qb == 0:
                    fw.dma("pool", dbg_z0, z_sb[zb][:, 0:256])
                t0 = 512 * qb
                seg = 0
                while seg < 512:
                    jdst = (t0 + seg) // TLOC
                    off = (t0 + seg) % TLOC
                    w_ = min(512 - seg, TLOC - off)
                    fw.dma("act", a2a_in[HPC * DH * jdst + DH * h:
                                         HPC * DH * jdst + DH * (h + 1),
                                         off:off + w_],
                           z_sb[zb][:, seg:seg + w_])
                    seg += w_
                deq_step(10)

        while deq_jobs or not deq_done[0]:
            deq_step()

        def a2a_fn(e):
            with allow_ici_all_to_all():
                return e.collective_compute(
                    "AllToAll", ALU.bypass, replica_groups=RG,
                    ins=[a2a_in.opt()], outs=[a2a_out.opt()])
        fw.cc("pool", a2a_fn, [a2a_in], [a2a_out])
      fw.phase_barrier()

      phC = ExitStack()
      with phC:
        ps_O = [psum(phC, f"ps_O{i}", [128, 512], F32) for i in range(4)]
        wo_bf = sbuf(phC, "wo_bf", [128, NH * D], BF16)
        z_t = [sbuf(phC, f"z_t{i}", [128, NH * 128], BF16) for i in range(2)]
        xc = [sbuf(phC, f"xc{i}", [128, D], F32) for i in range(2)]
        att_sb = sbuf(phC, "att_sb", [128, D], F32)
        x1_sb = sbuf(phC, "x1_sb", [128, D], F32)
        ln2_sb = sbuf(phC, "ln2_sb", [128, D], BF16)
        bn6c = sbuf(phC, "bn6c", [128, 24], F32)
        bn2c = sbuf(phC, "bn2c", [128, 2], F32)
        cve = sbuf(phC, "cve", [128, 1], F32)
        crs = sbuf(phC, "crs", [128, 1], F32)
        cnm = sbuf(phC, "cnm", [128, 1], F32)
        ln2F = sbuf(phC, "ln2F", [128, D], BF16)
        ps_T2 = [psum(phC, f"ps_T2{i}", [128, 128], BF16) for i in range(2)]

        fw.dma("pool", wo_bf.rearrange("p (h d) -> p h d", h=NH),
               wo_in.rearrange("h p d -> p h d"))

        for tt in range(NTT):
            b = tt % 2
            fw.dma("sp", z_t[b].rearrange("p (hh t) -> p hh t", hh=NH),
                   a2a_out.rearrange("(hh p) t -> p hh t", p=128)[
                       :, :, 128 * tt:128 * (tt + 1)])
            fw.dma("act", xc[b], x_in[128 * tt:128 * (tt + 1), :])
            for hh in range(NH):
                for dc in range(4):
                    fw.op("pe", lambda e, hh=hh, dc=dc, b=b: e.matmul(
                        ps_O[dc], z_t[b][:, 128 * hh:128 * (hh + 1)],
                        wo_bf[:, D * hh + 512 * dc:D * hh + 512 * (dc + 1)],
                        start=(hh == 0), stop=(hh == NH - 1)),
                        reads=[z_t[b], wo_bf], writes=[ps_O[dc]])
            for dc in range(4):
                fw.op("dve", lambda e, dc=dc: e.tensor_tensor(
                    out=att_sb[:, 512 * dc:512 * (dc + 1)], in0=ps_O[dc],
                    in1=bo_bc[:, 512 * dc:512 * (dc + 1)], op=ALU.add),
                    reads=[ps_O[dc], bo_bc], writes=[att_sb])
            ln_stats(att_sb, c_eps5, bn6c, bn2c, cve, crs, cnm)
            fw.op("dve", lambda e: e.tensor_scalar(out=x1_sb, in0=att_sb,
                  scalar1=crs, scalar2=cnm, op0=ALU.mult, op1=ALU.add),
                  reads=[att_sb, crs, cnm], writes=[x1_sb])
            fw.op("dve", lambda e, b=b: e.tensor_tensor(out=x1_sb, in0=x1_sb,
                  in1=xc[b], op=ALU.add), reads=[x1_sb, xc[b]], writes=[x1_sb])
            fw.dma("sp", x1_park[128 * tt:128 * (tt + 1), :], x1_sb)
            ln_stats(x1_sb, c_eps5, bn6c, bn2c, cve, crs, cnm)
            fw.op("dve", lambda e: e.tensor_scalar(out=ln2_sb, in0=x1_sb,
                  scalar1=crs, scalar2=cnm, op0=ALU.mult, op1=ALU.add),
                  reads=[x1_sb, crs, cnm], writes=[ln2_sb])
            for k in range(16):
                fw.op("pe", lambda e, k=k: e.transpose(
                    ps_T2[k % 2], ln2_sb[:, 128 * k:128 * (k + 1)], ident),
                    reads=[ln2_sb, ident], writes=[ps_T2[k % 2]])
                if k % 2 == 0:
                    fw.op("dve", lambda e, k=k: e.tensor_copy(
                        out=ln2F[:, 128 * k:128 * (k + 1)], in_=ps_T2[k % 2]),
                        reads=[ps_T2[k % 2]], writes=[ln2F])
                else:
                    fw.op("act", lambda e, k=k: e.copy(
                        out=ln2F[:, 128 * k:128 * (k + 1)], in_=ps_T2[k % 2]),
                        reads=[ps_T2[k % 2]], writes=[ln2F])
            fw.dma("act", ln2_park.rearrange("(k p) t -> p k t", p=128)[
                :, :, 128 * tt:128 * (tt + 1)],
                   ln2F.rearrange("p (k t) -> p k t", k=16))
      fw.phase_barrier()

      phDa = ExitStack()
      with phDa:
        h_fm = sbuf(phDa, "h_fm", [128, 48 * TLOC], BF16)
        phD = ExitStack()
        with phD:
            ln2f = sbuf(phD, "ln2f", [128, 16 * TLOC], BF16)
            gtmp = sbuf(phD, "gtmp", [128, 6 * TLOC], BF16)
            wb = [sbuf(phD, f"wb{i}", [128, 16 * 704], BF16) for i in range(2)]
            ps_G = [psum(phD, f"ps_G{i}", [128, TW], F32) for i in range(8)]

            for k in range(16):
                fw.dma("sp" if k % 2 == 0 else "act",
                       ln2f[:, TLOC * k:TLOC * (k + 1)],
                       ln2_park[128 * k:128 * (k + 1), :])

            def load_wbatch(buf, sec, half):
                for k in range(16):
                    r0 = 256 * (k // 2) + 128 * (k % 2)
                    c0 = half * II + 704 * sec
                    fw.dma("sp" if k % 2 == 0 else "act",
                           wb[buf][:, 704 * k:704 * (k + 1)],
                           ag3g_out[r0:r0 + 128, c0:c0 + 704])

            load_wbatch(0, 0, 0)
            step = 0
            for sec in range(8):
                for half in range(2):
                    buf = step % 2
                    nb = step + 1
                    if nb < 16:
                        load_wbatch(nb % 2, nb // 2, nb % 2)
                    for kk in range(6):
                        mrows = 128 if kk < 5 else 64
                        po = 2 * (kk % 4)
                        for k in range(16):
                            for t5 in range(NT5):
                                fw.op("pe", lambda e, kk=kk, t5=t5, k=k,
                                      buf=buf, mrows=mrows, po=po: e.matmul(
                                          ps_G[po + t5][0:mrows, :],
                                          wb[buf][:, 704 * k + 128 * kk:
                                                  704 * k + 128 * kk + mrows],
                                          ln2f[:, TLOC * k + TW * t5:
                                               TLOC * k + TW * (t5 + 1)],
                                          start=(k == 0), stop=(k == 15)),
                                      reads=[wb[buf], ln2f],
                                      writes=[ps_G[po + t5]])
                        ch = sec * 6 + kk
                        for t5 in range(NT5):
                            if half == 0:
                                fw.op("act", lambda e, kk=kk, t5=t5, po=po,
                                      mrows=mrows: e.activation(
                                          out=gtmp[0:mrows,
                                                   TLOC * kk + TW * t5:
                                                   TLOC * kk + TW * (t5 + 1)],
                                          in_=ps_G[po + t5][0:mrows, :],
                                          func=AF.Silu, bias=0.0, scale=1.0),
                                      reads=[ps_G[po + t5]], writes=[gtmp])
                            else:
                                fw.op("dve", lambda e, ch=ch, kk=kk, t5=t5,
                                      po=po, mrows=mrows: e.tensor_tensor(
                                          out=h_fm[0:mrows,
                                                   TLOC * ch + TW * t5:
                                                   TLOC * ch + TW * (t5 + 1)],
                                          in0=gtmp[0:mrows,
                                                   TLOC * kk + TW * t5:
                                                   TLOC * kk + TW * (t5 + 1)],
                                          in1=ps_G[po + t5][0:mrows, :],
                                          op=ALU.mult),
                                      reads=[gtmp, ps_G[po + t5]],
                                      writes=[h_fm])
                                if mrows == 64:
                                    fw.op("dve", lambda e, ch=ch, t5=t5:
                                          e.memset(h_fm[64:128,
                                                   TLOC * ch + TW * t5:
                                                   TLOC * ch + TW * (t5 + 1)],
                                                   0.0), writes=[h_fm])
                    step += 1
        fw.phase_barrier()

        phD2 = ExitStack()
        with phD2:
            ps_Y = [psum(phD2, f"ps_Y{i}", [128, 512], F32) for i in range(8)]
            wd = [sbuf(phD2, f"wd{i}", [128, 512], BF16) for i in range(8)]
            yst = sbuf(phD2, "yst", [128, 512], F32)
            yv = [sbuf(phD2, f"yv{i}", [128, D], F32) for i in range(4)]
            vv = sbuf(phD2, "vv", [128, D], F32)
            x1r = sbuf(phD2, "x1r", [128, D], F32)
            o_sb = sbuf(phD2, "o_sb", [128, D], F32)
            bn6d = sbuf(phD2, "bn6d", [128, 24], F32)
            bn2d = sbuf(phD2, "bn2d", [128, 2], F32)
            dve_c = sbuf(phD2, "dve_c", [128, 1], F32)
            drs = sbuf(phD2, "drs", [128, 1], F32)
            dnm = sbuf(phD2, "dnm", [128, 1], F32)
            dm2 = sbuf(phD2, "dm2", [128, 1], F32)

            def down_mms(tp):
                for qd in range(4):
                    for ic in range(48):
                        buf = ic % 8
                        blk = ic // 6
                        ii = 128 * (ic % 6)
                        fw.dma("sp" if ic % 2 == 0 else "act", wd[buf],
                               ag3d_out[768 * blk + ii:768 * blk + ii + 128,
                                        512 * qd:512 * (qd + 1)])
                        for tti in range(2):
                            tt = 2 * tp + tti
                            fw.op("pe", lambda e, ic=ic, tt=tt, buf=buf,
                                  tti=tti, qd=qd: e.matmul(
                                  ps_Y[4 * tti + qd],
                                  h_fm[:, TLOC * ic + 128 * tt:
                                       TLOC * ic + 128 * (tt + 1)],
                                  wd[buf], start=(ic == 0), stop=(ic == 47)),
                                  reads=[h_fm, wd[buf]],
                                  writes=[ps_Y[4 * tti + qd]])
                for tti in range(2):
                    tt = 2 * tp + tti
                    for qd in range(4):
                        fw.op("dve", lambda e, tti=tti, qd=qd, tt=tt:
                              e.tensor_copy(
                              out=yv[tt % 4][:, 512 * qd:512 * (qd + 1)],
                              in_=ps_Y[4 * tti + qd]),
                              reads=[ps_Y[4 * tti + qd]], writes=[yv[tt % 4]])

            def epilogue(tt):
                yvt = yv[tt % 4]
                fw.dma("act", x1r, x1_park[128 * tt:128 * (tt + 1), :])
                for c in range(4):
                    fw.op("dve", lambda e, c=c, yvt=yvt: e.bn_stats(
                        out=bn6d[:, 6 * c:6 * c + 6],
                        in_=yvt[:, 512 * c:512 * (c + 1)]),
                        reads=[yvt], writes=[bn6d])
                fw.op("dve", lambda e: e.bn_aggr(out=bn2d, in_=bn6d),
                      reads=[bn6d], writes=[bn2d])
                fw.op("dve", lambda e: e.scalar_tensor_tensor(out=dm2,
                      in0=bn2d[:, 0:1], scalar=bn2d[:, 0:1], in1=bn2d[:, 1:2],
                      op0=ALU.mult, op1=ALU.add), reads=[bn2d], writes=[dm2])
                fw.op("act", lambda e: e.activation(out=dve_c, in_=dm2,
                      func=AF.Ln, bias=c_eps6, scale=1.0),
                      reads=[dm2, c_eps6], writes=[dve_c])
                fw.op("act", lambda e: e.activation(out=drs, in_=dve_c,
                      func=AF.Exp, bias=0.0, scale=c_nh),
                      reads=[dve_c, c_nh], writes=[drs])
                fw.op("dve", lambda e, yvt=yvt: e.tensor_scalar(out=vv, in0=yvt,
                      scalar1=drs, scalar2=None, op0=ALU.mult),
                      reads=[yvt, drs], writes=[vv])
                for c in range(4):
                    fw.op("dve", lambda e, c=c: e.bn_stats(
                        out=bn6d[:, 6 * c:6 * c + 6],
                        in_=vv[:, 512 * c:512 * (c + 1)]),
                        reads=[vv], writes=[bn6d])
                fw.op("dve", lambda e: e.bn_aggr(out=bn2d, in_=bn6d),
                      reads=[bn6d], writes=[bn2d])
                fw.op("act", lambda e: e.activation(out=dve_c, in_=bn2d[:, 1:2],
                      func=AF.Ln, bias=c_eps5, scale=1.0),
                      reads=[bn2d, c_eps5], writes=[dve_c])
                fw.op("act", lambda e: e.activation(out=drs, in_=dve_c,
                      func=AF.Exp, bias=0.0, scale=c_nh),
                      reads=[dve_c, c_nh], writes=[drs])
                fw.op("dve", lambda e: e.scalar_tensor_tensor(out=dnm,
                      in0=bn2d[:, 0:1], scalar=-1.0, in1=drs,
                      op0=ALU.mult, op1=ALU.mult),
                      reads=[bn2d, drs], writes=[dnm])
                fw.op("dve", lambda e: e.tensor_scalar(out=o_sb, in0=vv,
                      scalar1=drs, scalar2=dnm, op0=ALU.mult, op1=ALU.add),
                      reads=[vv, drs, dnm], writes=[o_sb])
                fw.op("dve", lambda e: e.tensor_tensor(out=o_sb, in0=o_sb,
                      in1=x1r, op=ALU.add), reads=[o_sb, x1r], writes=[o_sb])
                fw.dma("sp", out_t[128 * tt:128 * (tt + 1), :], o_sb)

            NPAIR = NTT // 2
            down_mms(0)
            for tp in range(1, NPAIR):
                down_mms(tp)
                epilogue(2 * (tp - 1))
                epilogue(2 * (tp - 1) + 1)
            epilogue(2 * (NPAIR - 1))
            epilogue(2 * (NPAIR - 1) + 1)

      fw.wait_all("sp")
      fw.wait_all("act")
      fw.wait_all("pool")

      fw.prune_incs()
      sems = {s: es.enter_context(nc.semaphore("fws_" + s))
              for s in sorted(fw.sem_names)}
      with nc.Block() as block:
          @block.sync
          def _(e):
              fw.emit_queue("sp", e, sems)

          @block.vector
          def _(e):
              fw.emit_queue("dve", e, sems)

          @block.scalar
          def _(e):
              fw.emit_queue("act", e, sems)

          @block.gpsimd
          def _(e):
              fw.emit_queue("pool", e, sems)

          @block.tensor
          def _(e):
              fw.emit_queue("pe", e, sems)
    return nc


def shard_inputs(inputs, S):
    TLOC = S // NC
    x = np.asarray(inputs["L0_attn_x"], dtype=np.float32).reshape(S, D)
    wq = np.asarray(inputs["L0_attn_W_Q"], dtype=np.float32)
    wk = np.asarray(inputs["L0_attn_W_K"], dtype=np.float32)
    wv = np.asarray(inputs["L0_attn_W_V"], dtype=np.float32)
    wo = np.ascontiguousarray(np.asarray(inputs["L0_attn_W_O"], dtype=np.float32))
    bq = np.asarray(inputs["L0_attn_b_Q"], dtype=np.float32)
    bk = np.asarray(inputs["L0_attn_b_K"], dtype=np.float32)
    bv = np.asarray(inputs["L0_attn_b_V"], dtype=np.float32)
    bo = np.ascontiguousarray(
        np.asarray(inputs["L0_attn_b_O"], dtype=np.float32).reshape(1, D))
    gup = np.asarray(inputs["L0_ffn_gate_up_packed"], dtype=np.int32)
    gus = np.asarray(inputs["L0_ffn_gate_up_scales"], dtype=np.float16)
    dnp = np.asarray(inputs["L0_ffn_down_packed"], dtype=np.int32)
    dns = np.asarray(inputs["L0_ffn_down_scales"], dtype=np.float16)

    maps = []
    for c in range(NC):
        m = {}
        m["x_in"] = np.ascontiguousarray(x[TLOC * c:TLOC * (c + 1)])
        m["wq_in"] = np.ascontiguousarray(wq[HPC * c:HPC * (c + 1)])
        m["wk_in"] = np.ascontiguousarray(wk[HPC * c:HPC * (c + 1)])
        m["wv_in"] = np.ascontiguousarray(wv[HPC * c:HPC * (c + 1)])
        m["wo_in"] = wo
        m["bq_in"] = np.ascontiguousarray(bq[HPC * c:HPC * (c + 1)])
        m["bk_in"] = np.ascontiguousarray(bk[HPC * c:HPC * (c + 1)])
        m["bv_in"] = np.ascontiguousarray(bv[HPC * c:HPC * (c + 1)])
        m["bo_in"] = bo
        g = gup[:, GU_W * c:GU_W * (c + 1)].reshape(88, 128, GU_W)
        m["gup_in"] = np.ascontiguousarray(g.transpose(1, 0, 2))
        sc = gus[2 * c:2 * c + 2, :]
        e = sc.reshape(2, 88, 128).transpose(2, 1, 0)
        m["gus_in"] = np.ascontiguousarray(np.repeat(e, 128, axis=2))
        dd = dnp[:, DN_W * c:DN_W * (c + 1)].reshape(16, 128, DN_W)
        m["dnp_in"] = np.ascontiguousarray(dd.transpose(1, 0, 2))
        il = np.minimum(np.arange(768), 703)
        gidx = (704 * c + il) // 128
        ee = dns[gidx, :].reshape(768, 16, 128).transpose(2, 1, 0)
        m["dns_in"] = np.ascontiguousarray(ee)
        maps.append(m)
    return maps


def kernel(**inputs):
    S = int(np.asarray(inputs["L0_attn_x"]).shape[1])
    in_maps = shard_inputs(inputs, S)
    nc = build_program(S)
    res = run_bass_kernel_spmd(nc, in_maps, core_ids=list(range(NC)))
    out = np.concatenate([res.results[c]["out"] for c in range(NC)], axis=0)
    return out.reshape(1, S, D).astype(np.float32)


if __name__ == "__main__":
    nc = build_program(2048)
    n = sum(len(v) for v in nc.__dict__.get("_noexist", {})) if False else None
    print("built ok")
